# version 15
# baseline (speedup 1.0000x reference)
import contextlib
import numpy as np

import concourse.bass as bass
import concourse.mybir as mybir
from concourse.bass_utils import run_bass_kernel_spmd

F32 = mybir.dt.float32
BF16 = mybir.dt.bfloat16
AF = mybir.ActivationFunctionType
ALU = mybir.AluOpType
AX = mybir.AxisListType
ds = bass.ds

NC = 8
B, L, D, DI, NST, DTR = 2, 8192, 1024, 2048, 16, 64
TOK = B * L
TPC = TOK // NC
DSL = DI // NC
T1 = 512
W1 = T1 + 3
TM = 512
T4 = 256
NEXP = 8
FFN = 1536
EPS = 1e-5
BIG = 1.0e30
TC = 2048
NTC = TOK // TC

EXP_W = 3 * 1024 * FFN
MOE_TOT = (NEXP + 1) * EXP_W
MOE_SL = MOE_TOT // NC
WG_SZ = 1024 * FFN

AR_F = 24000
AR_B = 54000

ENGS = ("sync", "scalar", "vector", "tensor", "gpsimd")


class Prog:
    def __init__(self, nc):
        self.nc = nc
        self.es = contextlib.ExitStack()
        self.sems = {}
        self.cnt = {}
        self.steps = {e: [] for e in ENGS}

    def sem(self, name):
        if name not in self.sems:
            self.sems[name] = self.es.enter_context(self.nc.semaphore(name))
            self.cnt[name] = 0
        return self.sems[name]

    def step(self, eng, fn=None, waits=(), inc=None, inc_by=1):
        if inc is not None:
            self.sem(inc)
            self.cnt[inc] += inc_by
        for s, _ in waits:
            self.sem(s)
        self.steps[eng].append((fn, tuple(waits), inc, inc_by))

    def c(self, name):
        self.sem(name)
        return self.cnt[name]

    def after(self, *names):
        return [(n, self.c(n)) for n in names]

    def dma(self, eng, out, in_, waits=(), accum=None):
        q = "qy" if eng == "sync" else "qg"

        def f(e, out=out, in_=in_, accum=accum):
            if accum is None:
                return e.dma_start(out=out, in_=in_)
            return e.dma_start(out=out, in_=in_, accum_op=accum)
        self.step(eng, f, waits=waits, inc=q, inc_by=16)

    def dyndma(self, eng, fn, waits=()):
        q = "qy" if eng == "sync" else "qg"
        self.step(eng, fn, waits=waits, inc=q, inc_by=16)

    def mm(self, out, lhsT, rhs, start, stop, tp=None, waits=(), inc=None):
        def f(e, out=out, lhsT=lhsT, rhs=rhs, start=start, stop=stop, tp=tp):
            if tp is None:
                return e.matmul(out, lhsT, rhs, start=start, stop=stop)
            return e.matmul(out, lhsT, rhs, start=start, stop=stop,
                            tile_position=tp)
        self.step("tensor", f, waits=waits, inc=inc)

    def trans(self, out, in_, ident, waits=(), inc=None):
        def f(e, out=out, in_=in_, ident=ident):
            return e.transpose(out, in_, ident)
        self.step("tensor", f, waits=waits, inc=inc)

    def act(self, out, in_, func, bias=0.0, scale=1.0, waits=(), inc=None):
        def f(e, out=out, in_=in_, func=func, bias=bias, scale=scale):
            return e.activation(out, in_, func, bias=bias, scale=scale)
        self.step("scalar", f, waits=waits, inc=inc)

    def scopy(self, out, in_, waits=(), inc=None):
        def f(e, out=out, in_=in_):
            return e.copy(out, in_)
        self.step("scalar", f, waits=waits, inc=inc)

    def vop(self, name, *args, waits=(), inc=None):
        def f(e, name=name, args=args):
            return getattr(e, name)(*args)
        self.step("vector", f, waits=waits, inc=inc)

    def vwait(self, waits):
        self.step("vector", None, waits=waits)

    def emit(self, block):
        def make(eng_name):
            def body(eng):
                for fn, waits, inc, inc_by in self.steps[eng_name]:
                    for s, v in waits:
                        eng.wait_ge(self.sems[s], v)
                    ins = fn(eng) if fn is not None else None
                    if inc is not None:
                        if ins is None:
                            ins = eng.nop()
                        ins.then_inc(self.sems[inc], inc_by)
            return body
        block.sync(make("sync"))
        block.scalar(make("scalar"))
        block.vector(make("vector"))
        block.tensor(make("tensor"))
        block.gpsimd(make("gpsimd"))


class Bump:
    def __init__(self, ap, elems, name):
        self.ap, self.n, self.off, self.name = ap, elems, 0, name

    def take(self, elems):
        elems = int(elems)
        off = (self.off + 15) // 16 * 16
        v = self.ap[:, off:off + elems]
        self.off = off + elems
        assert self.off <= self.n, f"{self.name} overflow {self.off}>{self.n}"
        return v

    def mark(self):
        return self.off

    def reset(self, m=0):
        self.off = m


def ch(ap_, c, w):
    return ap_[:, c * w:(c + 1) * w]


def _core_offs(e):
    if not hasattr(e, "_mk_offs"):
        j = e.partition_id()
        jD0 = e.snap(j * DSL, min_val=0, max_val=(NC - 1) * DSL)
        jT0 = e.snap(j * TPC, min_val=0, max_val=(NC - 1) * TPC)
        e._mk_offs = (jD0, jT0)
    return e._mk_offs


def _core_j(e):
    if not hasattr(e, "_mk_j"):
        e._mk_j = e.snap(e.partition_id(), min_val=0, max_val=NC - 1)
    return e._mk_j


def build_program(debug=False):
    nc = bass.Bass("TRN2", target_bir_lowering=False)

    def din(name, shape, dt=F32):
        return nc.dram_tensor(name, shape, dt, kind="ExternalInput").ap()

    def dint(name, shape, dt=F32):
        return nc.dram_tensor(name, shape, dt).ap()

    x_fm = din("x_fm", [D, TPC + 3])
    w_ip = din("w_ip", [D, 2 * DI])
    nrm_col = din("nrm_col", [128, D // 128])
    cw_col = din("cw_col", [128, (DI // 128) * 4])
    cb_col = din("cb_col", [128, DI // 128])
    w_xp = din("w_xp", [DI, 320])
    w_dt = din("w_dt", [DTR, DSL])
    dtb_col = din("dtb_col", [128, DSL // 128])
    alog_pk = din("alog_pk", [128, DSL // 8])
    dp_col = din("dp_col", [128, DI // 128])
    w_op = din("w_op", [DI, D])
    w_f1 = din("w_f1", [D, D])
    b1_col = din("b1_col", [128, D // 128])
    w_f2 = din("w_f2", [D, D])
    b2_col = din("b2_col", [128, D // 128])
    lnw_col = din("lnw_col", [128, D // 128])
    lnb_col = din("lnb_col", [128, D // 128])
    w_gate = din("w_gate", [128, (D // 128) * 8])
    w_moe_sl = din("w_moe_sl", [MOE_SL // 2048, 2048], F32)
    sel4 = din("sel4", [128, 512], BF16)
    ysel_c = din("ysel_c", [128, 8], BF16)
    ones1 = din("ones1", [1, 128], F32)
    onesk = din("onesk", [128, 1], F32)
    ident = din("ident", [128, 128], F32)

    out_fm = nc.dram_tensor("out_fm", [D, TPC], F32, kind="ExternalOutput").ap()

    uc_agi = dint("uc_agi", [4, DI, T1], BF16)
    uc_ago = dint("uc_ago", [4, NC, DI, T1], BF16)
    bc_agi = dint("bc_agi", [4, 256, T1], F32)
    bc_ago = dint("bc_ago", [4, NC, 256, T1], F32)
    dtr_agi = dint("dtr_agi", [4, DTR, T1], BF16)
    dtr_ago = dint("dtr_ago", [4, NC, DTR, T1], BF16)
    wm_agi = dint("wm_agi", [MOE_SL // 2048, 2048], BF16)
    wm_ago = dint("wm_ago", [NC, MOE_SL], BF16)
    y_agi = dint("y_agi", [NTC, DSL, TC], F32)
    y_ago = dint("y_ago", [NTC, NC, DSL, TC], F32)
    uc_loc = dint("uc_loc", [4, NC, DSL, T1], BF16)
    res_dr = dint("res_dr", [DI, TPC], F32)
    xn_dr = dint("xn_dr", [D, TPC], F32)
    xnb_dr = dint("xnb_dr", [D, TPC], BF16)
    comb_dr = dint("comb_dr", [NEXP, TPC], F32)
    moe_dr = dint("moe_dr", [D, TPC], F32)
    y_loc = dint("y_loc", [NC, DSL, TPC], F32)

    wm_flat = wm_ago.rearrange("a b -> (a b)")

    def moe_w(off, k, m):
        return wm_flat[off:off + k * m].rearrange("(k m) -> k m", m=m)

    es = contextlib.ExitStack()

    def sb(name, shape, dt):
        return es.enter_context(nc.sbuf_tensor(name, shape, dt)).ap()

    nrm = sb("nrm", [128, D // 128], F32)
    cw = sb("cw", [128, (DI // 128) * 4], F32)
    cbx = sb("cbx", [128, DI // 128], F32)
    dtb = sb("dtb", [128, DSL // 128], F32)
    aneg = sb("aneg", [128, DSL // 8], F32)
    dpc = sb("dpc", [128, DI // 128], F32)
    b1c = sb("b1c", [128, D // 128], F32)
    b2c = sb("b2c", [128, D // 128], F32)
    lnw = sb("lnw", [128, D // 128], F32)
    lnb = sb("lnb", [128, D // 128], F32)
    sel = sb("sel", [128, 512], BF16)
    ysl = sb("ysl", [128, 8], BF16)
    on1 = sb("on1", [1, 128], F32)
    onk = sb("onk", [128, 1], F32)
    idn = sb("idn", [128, 128], F32)
    wgt8 = sb("wgt8", [128, (D // 128) * 8], F32)
    wdtb = sb("wdtb", [DTR, DSL], BF16)
    hst = sb("hst", [128, DSL // 8], F32)
    drn = sb("drn", [128, 2], F32)

    arf_t = sb("arf", [128, AR_F], F32)
    arb_t = sb("arb", [128, AR_B], BF16)
    arf = Bump(arf_t, AR_F, "arf")
    arb = Bump(arb_t, AR_B, "arb")

    cst_eps = sb("cst_eps", [128, 1], F32)
    nc.const_aps.register(F32, EPS, cst_eps)

    pb = [es.enter_context(nc.psum_tensor(f"pb{i}", [128, 512], F32)).ap()
          for i in range(8)]

    P = Prog(nc)

    def wall_all():
        w = P.after("qy", "qg", "s", "v", "t")
        for engn in ENGS:
            P.step(engn, None, waits=w)

    taps = []
    if debug:
        for nm, src, shape, dt in (
                ("xn", xn_dr, [D, TPC], F32),
                ("uc", uc_agi, [4, DI, T1], BF16),
                ("bc", bc_agi, [4, 256, T1], F32),
                ("dtr", dtr_agi, [4, DTR, T1], BF16),
                ("res", res_dr, [DI, TPC], F32),
                ("comb", comb_dr, [NEXP, TPC], F32),
                ("y", y_agi, [NTC, DSL, TC], F32),
                ("moe", moe_dr, [D, TPC], F32),
                ("xnb", xnb_dr, [D, TPC], BF16)):
            t = nc.dram_tensor("tap_" + nm, shape, dt, kind="ExternalOutput").ap()
            taps.append((t, src))

    x_fm_v = x_fm.rearrange("(c p) t -> c p t", p=128)
    uc_agi_v = uc_agi.rearrange("q (c p) t -> q c p t", p=128)
    res_dr_v = res_dr.rearrange("(c p) t -> c p t", p=128)
    xn_dr_v = xn_dr.rearrange("(c p) t -> c p t", p=128)
    xnb_dr_v = xnb_dr.rearrange("(c p) t -> c p t", p=128)
    moe_dr_v = moe_dr.rearrange("(c p) t -> c p t", p=128)
    out_fm_v = out_fm.rearrange("(c p) t -> c p t", p=128)

    for o_, i_ in ((nrm, nrm_col), (cw, cw_col), (cbx, cb_col), (dtb, dtb_col),
                   (dpc, dp_col), (b1c, b1_col), (b2c, b2_col), (lnw, lnw_col),
                   (lnb, lnb_col), (sel, sel4), (ysl, ysel_c), (on1, ones1),
                   (onk, onesk), (idn, ident), (aneg, alog_pk), (wgt8, w_gate)):
        P.dma("sync", o_, i_)
    n_const = P.c("qy")

    P.vop("memset", cst_eps, EPS, inc="v")
    P.act(aneg, aneg, AF.Exp, waits=[("qy", n_const)], inc="s")
    P.vop("tensor_scalar_mul", aneg, aneg, -1.0, waits=P.after("s"), inc="v")

    wipb = arb.take(8 * 2 * DI)
    wxpb = arb.take(16 * 320)
    MB_P1 = arb.mark()
    wmb = arb.take(2048)
    wstg = arf.take(2048)

    w_ip_v = w_ip.rearrange("(c p) m -> c p m", p=128)
    w_xp_v = w_xp.rearrange("(c p) m -> c p m", p=128)

    for c in range(8):
        for h in range(2):
            P.dma("sync", wstg[:, 0:2048], w_ip_v[c, :, h * 2048:(h + 1) * 2048],
                  waits=P.after("v"))
            P.vop("tensor_scalar_mul",
                  wipb[:, c * 4096 + h * 2048:c * 4096 + (h + 1) * 2048],
                  wstg[:, 0:2048], nrm[:, c:c + 1],
                  waits=P.after("qy"), inc="v")
    for c in range(16):
        P.dma("sync", wstg[:, 0:320], w_xp_v[c], waits=P.after("s", "v"))
        P.scopy(ch(wxpb, c, 320), wstg[:, 0:320], waits=P.after("qy"), inc="s")
    P.dma("sync", wstg[0:DTR, 0:DSL], w_dt, waits=P.after("s"))
    P.scopy(wdtb, wstg[0:DTR, 0:DSL], waits=P.after("qy"), inc="s")

    n_rows = MOE_SL // 2048
    for r0 in range(0, n_rows, 128):
        nr = min(128, n_rows - r0)
        P.dma("sync", wstg[0:nr, 0:2048], w_moe_sl[r0:r0 + nr],
              waits=P.after("qg", "s"))
        P.scopy(wmb[0:nr, :], wstg[0:nr, 0:2048], waits=P.after("qy"), inc="s")
        P.dma("gpsimd", wm_agi[r0:r0 + nr], wmb[0:nr, :], waits=P.after("s"))

    def collective(kind, ins, outs, waits=()):
        def f(e, kind=kind, ins=ins, outs=outs):
            return e.collective_compute(
                kind, ALU.bypass, replica_groups=[list(range(NC))],
                ins=[ins.opt()], outs=[outs.opt()])
        P.step("gpsimd", f, waits=waits, inc="cc")

    collective("AllGather", wm_agi, wm_ago, waits=P.after("qg"))
    CC_WMOE = P.c("cc")
    wall_all()

    arf.reset(0)
    logsb_a = arf.take(TPC)
    logsb = logsb_a[0:8, :]
    MF_P1 = arf.mark()
    xt = arf.take(8 * W1)
    xnf = arf.take(8 * W1)
    ut = arf.take(16 * W1)
    convo = arf.take(T1)
    resst = arf.take(2 * W1)
    statr_a = arf.take(4 * W1)
    statr = statr_a[0:1, :]
    arb.reset(MB_P1)
    xn2b = arb.take(8 * W1)
    ucb = arb.take(16 * T1)
    xnbst = ucb[:, 0:8 * T1]
    dtrst = arb.take(T1)

    S_MU, S_RSTD, S_MR, S_V = 0, 1, 2, 3

    def srow(slot, off, n):
        return statr[:, slot * W1 + off:slot * W1 + off + n]

    for it in range(TPC // T1):
        t0 = it * T1
        halo = (it == 0)
        xo = 0 if halo else 3
        nnew = W1 if halo else T1
        src_off = t0 if halo else t0 + 3
        if not halo:
            uv = ut.rearrange("p (c w) -> p c w", w=W1)
            P.vop("tensor_copy", uv[:, :, 0:3], uv[:, :, T1:T1 + 3],
                  waits=P.after("t", "s"), inc="v")
        for c in range(8):
            w0 = ()
            if c == 0:
                w0 = ([("qy", n_const)] if it == 0
                      else P.after("v", "s", "t"))
            P.dma("sync", ch(xt, c, W1)[:, xo:xo + nnew],
                  x_fm_v[c, :, src_off:src_off + nnew], waits=w0)
        ldx = P.c("qy")

        subs = [(0, 512), (3, 512)] if halo else [(3, 512)]
        for off, n in subs:
            for c in range(8):
                w0 = ()
                if c == 0:
                    w0 = ([("qy", ldx)] if it == 0 and off == 0
                          else P.after("qy", "v", "s"))
                P.mm(pb[7][0:1, 0:n], onk, ch(xt, c, W1)[:, off:off + n],
                     start=(c == 0), stop=(c == 7), waits=w0,
                     inc=("t" if c == 7 else None))
            for c in range(8):
                P.act(convo[:, 0:n], ch(xt, c, W1)[:, off:off + n], AF.Square,
                      waits=P.after("t"), inc="s")
                P.mm(pb[6][0:1, 0:n], onk, convo[:, 0:n],
                     start=(c == 0), stop=(c == 7), waits=P.after("s"),
                     inc="t")
            mu, rstd = srow(S_MU, off, n), srow(S_RSTD, off, n)
            mr, vv = srow(S_MR, off, n), srow(S_V, off, n)
            P.vop("tensor_scalar_mul", mu, pb[7][0:1, 0:n], 1.0 / D,
                  waits=P.after("t"), inc="v")
            P.vop("tensor_tensor", vv, mu, mu, ALU.mult, inc="v")
            P.vop("scalar_tensor_tensor", vv, pb[6][0:1, 0:n], 1.0 / D, vv,
                  ALU.mult, ALU.subtract, inc="v")
            P.act(rstd, vv, AF.Ln, bias=EPS, waits=P.after("v"), inc="s")
            P.act(rstd, rstd, AF.Exp, scale=-0.5, inc="s")
            P.vop("tensor_tensor", mr, mu, rstd, ALU.mult,
                  waits=P.after("s"), inc="v")
            P.mm(pb[5][:, 0:n], on1, rstd, True, True, waits=P.after("v"), inc="t")
            P.mm(pb[4][:, 0:n], on1, mr, True, True, inc="t")
            for c in range(8):
                P.vop("tensor_tensor", ch(xnf, c, W1)[:, off:off + n],
                      ch(xt, c, W1)[:, off:off + n], pb[5][:, 0:n], ALU.mult,
                      waits=(P.after("t", "qy") if c == 0 else ()), inc="v")
                P.vop("tensor_tensor", ch(xnf, c, W1)[:, off:off + n],
                      ch(xnf, c, W1)[:, off:off + n], pb[4][:, 0:n],
                      ALU.subtract, inc="v")
            for c in range(8):
                P.act(convo[:, 0:n], ch(xnf, c, W1)[:, off:off + n], AF.Square,
                      waits=P.after("v", "t"), inc="s")
                P.mm(pb[6][0:1, 0:n], onk, convo[:, 0:n],
                     start=(c == 0), stop=(c == 7), waits=P.after("s"),
                     inc="t")
            r2 = srow(S_RSTD, off, n)
            P.act(r2, pb[6][0:1, 0:n], AF.Ln, bias=EPS, scale=1.0 / D,
                  waits=P.after("t"), inc="s")
            P.act(r2, r2, AF.Exp, scale=-0.5, inc="s")
            P.mm(pb[5][:, 0:n], on1, r2, True, True,
                 waits=P.after("s", "v"), inc="t")
            for c in range(8):
                P.vop("tensor_tensor", ch(xn2b, c, W1)[:, off:off + n],
                      ch(xnf, c, W1)[:, off:off + n], pb[5][:, 0:n], ALU.mult,
                      waits=(P.after("t") if c == 0 else ()), inc="v")
            P.vop("tensor_copy", drn[:, 0:1],
                  ch(xn2b, 7, W1)[:, off + n - 1:off + n], inc="v")

        P.scopy(xnbst.rearrange("p (c w) -> p c w", w=T1),
                xnf.rearrange("p (c w) -> p c w", w=W1)[:, :, 3:3 + T1],
                waits=P.after("v", "qy"), inc="s")
        for c in range(8):
            P.dma("sync", xn_dr_v[c, :, t0:t0 + T1],
                  ch(xnf, c, W1)[:, 3:3 + T1],
                  waits=(P.after("v") if c == 0 else ()))
            P.dma("sync", xnb_dr_v[c, :, t0:t0 + T1], ch(xnbst, c, T1),
                  waits=(P.after("s") if c == 0 else ()))
        for c in range(8):
            P.mm(pb[7][0:8, 0:T1], wgt8[:, c * 8:(c + 1) * 8],
                 ch(xnf, c, W1)[:, 3:3 + T1], start=(c == 0), stop=(c == 7),
                 waits=(P.after("v") if c == 0 else ()),
                 inc=("t" if c == 7 else None))
        P.vop("tensor_copy", logsb[:, t0:t0 + T1], pb[7][0:8, 0:T1],
              waits=P.after("t"), inc="v")

        for m in range(32):
            bk = pb[5 + (m % 2)]
            for off, n in subs:
                for k in range(8):
                    P.mm(bk[:, 0:n],
                         wipb[:, k * 4096 + m * 128:k * 4096 + (m + 1) * 128],
                         ch(xn2b, k, W1)[:, off:off + n],
                         start=(k == 0), stop=(k == 7),
                         waits=(P.after("v", "s") if k == 0 else ()),
                         inc=("t" if k == 7 else None))
                if m < 16:
                    P.vop("tensor_copy", ch(ut, m, W1)[:, off:off + n],
                          bk[:, 0:n], waits=P.after("t"), inc="v")
                else:
                    half = 0 if (m % 2 == 0) else W1
                    P.scopy(resst[:, half + off:half + off + n],
                            bk[:, 0:n], waits=P.after("t", "qy"), inc="s")
            if m >= 16:
                half = 0 if (m % 2 == 0) else W1
                P.dma("sync", res_dr_v[m - 16, :, t0:t0 + T1],
                      resst[:, half + 3:half + 3 + T1], waits=P.after("s"))

        for c in range(16):
            P.vop("tensor_scalar_mul", convo, ch(ut, c, W1)[:, 0:T1],
                  cw[:, 4 * c:4 * c + 1], waits=P.after("v", "s"), inc="v")
            for k in (1, 2, 3):
                P.vop("scalar_tensor_tensor", convo,
                      ch(ut, c, W1)[:, k:k + T1], cw[:, 4 * c + k:4 * c + k + 1],
                      convo, ALU.mult, ALU.add, inc="v")
            P.act(ch(ucb, c, T1), convo, AF.Silu, bias=cbx[:, c:c + 1],
                  waits=P.after("v", "qy"), inc="s")
            P.dma("sync", uc_agi_v[it, c], ch(ucb, c, T1),
                  waits=P.after("s"))

        for mi, (mo, msz, bank) in enumerate(((0, 64, 5), (64, 128, 6),
                                              (192, 128, 5))):
            bk = pb[bank]
            for k in range(16):
                P.mm(bk[0:msz, 0:T1], wxpb[:, k * 320 + mo:k * 320 + mo + msz],
                     ch(ucb, k, T1), start=(k == 0), stop=(k == 15),
                     waits=(P.after("s", "v") if k == 0 else ()),
                     inc=("t" if k == 15 else None))
            if mi == 0:
                P.scopy(dtrst[0:DTR, 0:T1], bk[0:64, 0:T1],
                        waits=P.after("t", "qy"), inc="s")
                P.dma("sync", dtr_agi[it], dtrst[0:DTR, 0:T1],
                      waits=P.after("s"))
            else:
                half = 0 if mi == 1 else W1
                P.vop("tensor_copy", resst[:, half:half + T1], bk[0:128, 0:T1],
                      waits=P.after("t", "qy"), inc="v")
                P.dma("sync", bc_agi[it, (mi - 1) * 128:mi * 128, :],
                      resst[:, half:half + T1], waits=P.after("v"))

    q_p1_done = P.c("qy")
    cc_scan_in = {}

    arf.reset(MF_P1)
    tokm = arf.take(128)
    l2m = arf.take(128)
    is1m = arf.take(128)
    is2m = arf.take(128)
    m1r = arf.take(16)
    m2r = arf.take(16)
    qr = arf.take(16)
    w1r = arf.take(16)
    w2r = arf.take(16)
    combfm_a = arf.take(TPC)
    combfm = combfm_a[0:8, :]

    for c in range(16):
        P.mm(pb[7][:, 8 * c:8 * c + 8], logsb[:, 128 * c:128 * (c + 1)],
             idn[0:8, 0:8], True, True,
             waits=(P.after("s", "v") if c == 0 else ()),
             inc=("t" if c == 15 else None))
    P.vop("tensor_copy", tokm, pb[7][:, 0:128], waits=P.after("t"), inc="v")
    tv = tokm.rearrange("p (c e) -> p c e", e=8)
    P.vop("tensor_reduce", m1r, tv, AX.X, ALU.max, inc="v")
    m1b = m1r.rearrange("p (c o) -> p c o", o=1).broadcast_to([128, 16, 8])
    P.vop("tensor_tensor", is1m.rearrange("p (c e) -> p c e", e=8), tv, m1b,
          ALU.is_equal, inc="v")
    P.vop("scalar_tensor_tensor", l2m, is1m, -BIG, tokm, ALU.mult, ALU.add,
          inc="v")
    P.vop("tensor_reduce", m2r, l2m.rearrange("p (c e) -> p c e", e=8), AX.X,
          ALU.max, inc="v")
    m2b = m2r.rearrange("p (c o) -> p c o", o=1).broadcast_to([128, 16, 8])
    P.vop("tensor_tensor", is2m.rearrange("p (c e) -> p c e", e=8),
          l2m.rearrange("p (c e) -> p c e", e=8), m2b, ALU.is_equal, inc="v")
    P.vop("tensor_tensor", qr, m2r, m1r, ALU.subtract, inc="v")
    P.act(w1r, qr, AF.Sigmoid, scale=-1.0, waits=P.after("v"), inc="s")
    P.act(w2r, qr, AF.Sigmoid, scale=1.0, inc="s")
    P.vwait(P.after("s"))
    w1b = w1r.rearrange("p (c o) -> p c o", o=1).broadcast_to([128, 16, 8])
    w2b = w2r.rearrange("p (c o) -> p c o", o=1).broadcast_to([128, 16, 8])
    P.vop("tensor_tensor", is1m.rearrange("p (c e) -> p c e", e=8),
          is1m.rearrange("p (c e) -> p c e", e=8), w1b, ALU.mult, inc="v")
    P.vop("tensor_tensor", is2m.rearrange("p (c e) -> p c e", e=8),
          is2m.rearrange("p (c e) -> p c e", e=8), w2b, ALU.mult, inc="v")
    P.vop("tensor_tensor", is1m, is1m, is2m, ALU.add, inc="v")
    for cq in range(4):
        for j in range(4):
            c = 4 * cq + j
            P.mm(pb[5][0:8, 128 * j:128 * (j + 1)],
                 is1m[:, 8 * c:8 * c + 8], idn, True, True,
                 waits=(P.after("v", "s") if j == 0 else ()),
                 inc=("t" if j == 3 else None))
        P.vop("tensor_copy", combfm[:, 512 * cq:512 * (cq + 1)],
              pb[5][0:8, 0:512], waits=P.after("t"), inc="v")
    P.dma("gpsimd", comb_dr, combfm, waits=P.after("v"))

    wall_all()

    arf.reset(0)
    q1m = arf.take(TM)
    q1m2 = arf.take(TM)
    q1m3 = arf.take(TM)
    acc2 = [arf.take(8 * TM), arf.take(8 * TM)]
    combsb2 = [arf.take(TM), arf.take(TM)]
    arb.reset(0)
    xnb2 = [arb.take(8 * TM), arb.take(8 * TM)]
    wgb = arb.take(8 * FFN)
    wub = arb.take(8 * FFN)
    wdb = arb.take(12 * 1024)
    gsil = arb.take(TM)
    gsil2 = arb.take(TM)
    gsil3 = arb.take(TM)
    t2b = arb.take(12 * TM)

    s_cnt_gs = {}
    v_cnt_t2 = {}
    v_cnt_acc = {}
    t_gu_done = None
    t_dn_done = None
    q_str = None
    for half in range(2):
        for ti in range(2):
            tt = half * 2 + ti
            t0 = tt * TM
            for c in range(8):
                w0 = ()
                if ti == 0 and c == 0:
                    w0 = ([("t", t_gu_done)] if t_gu_done is not None
                          else P.after("t"))
                P.dma("sync", ch(xnb2[ti], c, TM), xnb_dr_v[c, :, t0:t0 + TM],
                      waits=w0)
        for e in range(NEXP + 1):
            off = e * EXP_W
            wg2 = moe_w(off, 1024, FFN).rearrange("(c p) m -> c p m", p=128)
            wu2 = moe_w(off + WG_SZ, 1024, FFN).rearrange(
                "(c p) m -> c p m", p=128)
            wd2 = moe_w(off + 2 * WG_SZ, FFN, 1024).rearrange(
                "(c p) m -> c p m", p=128)
            wts = ([("t", t_gu_done)] if t_gu_done is not None
                   else P.after("t")) + [("cc", CC_WMOE)]
            for k in range(8):
                P.dma("sync", ch(wgb, k, FFN), wg2[k],
                      waits=(wts if k == 0 else ()))
                P.dma("sync", ch(wub, k, FFN), wu2[k])
            for k in range(12):
                P.dma("sync", ch(wdb, k, 1024), wd2[k],
                      waits=([("t", t_dn_done)]
                             if (k == 0 and t_dn_done is not None) else ()))
            if e < NEXP:
                for ti in range(2):
                    tt = half * 2 + ti
                    P.dma("sync", combsb2[ti],
                          comb_dr[e:e + 1, tt * TM:(tt + 1) * TM]
                          .broadcast_to([128, TM]),
                          waits=(P.after("v") if ti == 0 else ()))
            ldt = P.c("qy")
            for ti in range(2):
                for mf in range(12):
                    par = mf % 3
                    bg, bu = ((pb[0], pb[3]), (pb[4], pb[5]),
                              (pb[6], pb[7]))[par]
                    w_first = [("qy", ldt)]
                    if s_cnt_gs.get(par) is not None:
                        w_first.append(("s", s_cnt_gs[par]))
                        w_first.append(("v", v_cnt_t2[par]))
                    for k in range(8):
                        P.mm(bg[:, 0:TM],
                             wgb[:, k * FFN + mf * 128:k * FFN + (mf + 1) * 128],
                             ch(xnb2[ti], k, TM), start=(k == 0), stop=(k == 7),
                             waits=(w_first if k == 0 else ()))
                    for k in range(8):
                        P.mm(bu[:, 0:TM],
                             wub[:, k * FFN + mf * 128:k * FFN + (mf + 1) * 128],
                             ch(xnb2[ti], k, TM), start=(k == 0), stop=(k == 7),
                             inc=("t" if k == 7 else None))
                    gs = (gsil, gsil2, gsil3)[par]
                    q1 = (q1m, q1m2, q1m3)[par]
                    P.act(gs, bg[:, 0:TM], AF.Silu, waits=P.after("t"), inc="s")
                    s_cnt_gs[par] = P.c("s")
                    P.vop("tensor_tensor", q1, gs, bu[:, 0:TM], ALU.mult,
                          waits=P.after("s"), inc="v")
                    if e < NEXP:
                        P.vop("tensor_tensor", ch(t2b, mf, TM), q1,
                              combsb2[ti], ALU.mult, waits=[("qy", ldt)],
                              inc="v")
                    else:
                        P.vop("tensor_copy", ch(t2b, mf, TM), q1, inc="v")
                    v_cnt_t2[par] = P.c("v")
                P.vop("tensor_copy", drn[:, 0:1],
                      t2b[:, 12 * TM - 1:12 * TM], inc="v")
                v_all_t2 = P.c("v")
                if ti == 1:
                    t_gu_done = P.c("t")
                for mo in range(8):
                    par = mo % 2
                    bk = pb[1 + par]
                    w_first = [("v", v_all_t2)]
                    if v_cnt_acc.get(par) is not None:
                        w_first.append(("v", v_cnt_acc[par]))
                    for kc in range(12):
                        P.mm(bk[:, 0:TM],
                             wdb[:, kc * 1024 + mo * 128:kc * 1024 + (mo + 1) * 128],
                             ch(t2b, kc, TM), start=(kc == 0), stop=(kc == 11),
                             waits=(w_first if kc == 0 else ()),
                             inc=("t" if kc == 11 else None))
                    wv = [("t", P.c("t"))]
                    if e == 0 and mo == 0 and ti == 0 and q_str is not None:
                        wv.append(("qg", q_str))
                    if e == 0:
                        P.vop("tensor_copy", ch(acc2[ti], mo, TM),
                              bk[:, 0:TM], waits=wv, inc="v")
                    else:
                        P.vop("tensor_tensor", ch(acc2[ti], mo, TM),
                              ch(acc2[ti], mo, TM), bk[:, 0:TM], ALU.add,
                              waits=wv, inc="v")
                    v_cnt_acc[par] = P.c("v")
                if ti == 1:
                    t_dn_done = P.c("t")
            if half == 0:
                if e in (1, 3, 5, 7):
                    t = e // 2
                    collective("AllGather", uc_agi[t], uc_ago[t],
                               waits=([("qy", q_p1_done)] if t == 0 else ()))
                    cc_scan_in[("uc", t)] = P.c("cc")
                elif e == 8:
                    collective("AllGather", bc_agi[0], bc_ago[0])
                    collective("AllGather", dtr_agi[0], dtr_ago[0])
            elif e in (0, 1, 2):
                collective("AllGather", bc_agi[e + 1], bc_ago[e + 1])
                collective("AllGather", dtr_agi[e + 1], dtr_ago[e + 1])
        v_done = P.c("v")
        for ti in range(2):
            tt = half * 2 + ti
            for mo in range(8):
                P.dma("gpsimd", moe_dr_v[mo, :, tt * TM:(tt + 1) * TM],
                      ch(acc2[ti], mo, TM),
                      waits=([("v", v_done)] if (ti == 0 and mo == 0) else ()))
        q_str = P.c("qg")

    CC_SCANIN = P.c("cc")

    for t in range(4):
        def ucopy(e_, t=t):
            jD0, _ = _core_offs(e_)
            return e_.dma_start(out=uc_loc[t],
                                in_=uc_ago[t][:, ds(jD0, DSL), :])
        P.dyndma("sync", ucopy, waits=[("cc", cc_scan_in[("uc", t)])])
    wall_all()

    arf.reset(0)
    delta = arf.take(2 * TC)
    spa = arf.take(512)
    spb = arf.take(512)
    brep = arf.take(TC)
    crep = arf.take(TC)
    a_t2 = [arf.take(TC + 16), arf.take(TC + 16)]
    b_t = arf.take(TC + 16)
    h_t2 = [arf.take(TC + 16), arf.take(TC + 16)]
    spc = arf.take(16)
    yst = arf.take(512)
    arb.reset(0)
    ucs = arb.take(2 * TC)
    delb = arb.take(2 * TC)
    dub = arb.take(2 * TC)
    ypb2 = [arb.take(TC), arb.take(TC)]
    dtrsl_a = arb.take(TC)
    dtrsl = dtrsl_a[0:DTR, :]

    wstg4 = arf.take(2048)
    wopb = arb.take(16 * 1024)
    wf1b = arb.take(8 * 1024)
    wf2b = arb.take(8 * 1024)
    w_op_v = w_op.rearrange("(c p) m -> c p m", p=128)
    w_f1_v = w_f1.rearrange("(c p) m -> c p m", p=128)
    w_f2_v = w_f2.rearrange("(c p) m -> c p m", p=128)
    w4q = ([(wopb, w_op_v, c) for c in range(16)]
           + [b for c in range(8)
              for b in ((wf1b, w_f1_v, c), (wf2b, w_f2_v, c))])
    w4_state = {"n": 0}
    w4_scopy = []
    ws_free = [None, None]

    def w4_dma(k):
        for _ in range(k):
            n = w4_state["n"]
            if n >= len(w4q):
                return
            dstb, srcv, c = w4q[n]
            par = n % 2
            stg = wstg4[:, par * 1024:(par + 1) * 1024]
            w = [("s", ws_free[par])] if ws_free[par] is not None else []
            P.dma("sync", stg, srcv[c], waits=w)
            w4_scopy.append((ch(dstb, c, 1024), stg, P.c("qy"), par))
            w4_state["n"] = n + 1

    def w4_flush():
        for dst, stg, q, par in w4_scopy:
            P.scopy(dst, stg, waits=[("qy", q)], inc="s")
            ws_free[par] = P.c("s")
        w4_scopy.clear()

    P.vop("memset", a_t2[0][:, TC:TC + 1], 0.0, waits=P.after("s", "t"), inc="v")
    P.vop("memset", a_t2[1][:, TC:TC + 1], 0.0, inc="v")
    P.vop("memset", b_t[:, TC:TC + 1], 0.0, inc="v")
    pend_g = None

    db_banks = [pb[5], pb[3], pb[4]]
    du_banks = [pb[6], pb[2], pb[1], pb[0]]
    s_exp = {i: None for i in range(3)}
    v_bt = {i: None for i in range(4)}
    v_scan = {0: None, 1: None}
    s_yev = None
    q_ydma = None
    subidx = 0
    pend_ysel = None

    def emit_ysel(g, tc, v_ym):
        nonlocal s_yev, q_ydma
        for su in range(4):
            w = [("v", P.c("v"))] if su == 0 else []
            if su == 0 and s_yev is not None:
                w.append(("s", s_yev))
            P.mm(pb[7][32 * su:32 * su + 8, 0:512], ysl,
                 ypb2[g % 2][:, su * 512:(su + 1) * 512],
                 True, True, tp=(0, 32 * su), waits=w,
                 inc=("t" if su == 3 else None))
        for su in range(4):
            wq = []
            if su == 0:
                wq = P.after("t")
                if q_ydma is not None:
                    wq.append(("qy", q_ydma))
            P.scopy(yst[32 * su:32 * su + 8, 0:512],
                    pb[7][32 * su:32 * su + 8, 0:512], waits=wq, inc="s")
        s_yev = P.c("s")
        for su in range(4):
            P.dma("sync",
                  y_agi[tc, 8 * g:8 * g + 8, su * 512:(su + 1) * 512],
                  yst[32 * su:32 * su + 8, 0:512],
                  waits=(P.after("s") if su == 0 else ()))
        q_ydma = P.c("qy")

    for tc in range(NTC):
        for t in range(4):
            w0 = ()
            if t == 0:
                w0 = P.after("s", "t")
                if tc == 0:
                    w0 = w0 + [("cc", CC_SCANIN)]
            P.dma("sync", dtrsl[:, t * T1:(t + 1) * T1], dtr_ago[t, tc],
                  waits=w0)
        for dd in range(2):
            for t in range(4):
                P.dma("sync", ch(ucs, dd, TC)[:, t * T1:(t + 1) * T1],
                      uc_loc[t, tc, dd * 128:(dd + 1) * 128, :],
                      waits=(P.after("v") if (dd == 0 and t == 0) else ()))
        for t in range(4):
            P.dma("sync", brep[:, t * T1:(t + 1) * T1], bc_ago[t, tc, 0:128, :],
                  waits=(P.after("v") if t == 0 else ()))
            P.dma("sync", crep[:, t * T1:(t + 1) * T1],
                  bc_ago[t, tc, 128:256, :])
        w4_flush()
        ld0 = P.c("qy")
        w4_dma(2)

        for dd in range(2):
            for su in range(4):
                par = subidx % 3
                subidx += 1
                bk = db_banks[par]
                w = [("qy", ld0)]
                if s_exp[par] is not None:
                    w.append(("s", s_exp[par]))
                w += P.after("v")
                P.mm(bk[:, 0:512], wdtb[:, dd * 128:(dd + 1) * 128],
                     dtrsl[:, su * 512:(su + 1) * 512], True, True,
                     waits=w, inc="t")
                dcol = delta[:, dd * TC + su * 512: dd * TC + (su + 1) * 512]
                P.act(spa, bk[:, 0:512], AF.Abs, bias=dtb[:, dd:dd + 1],
                      waits=P.after("t"), inc="s")
                P.act(spa, spa, AF.Exp, scale=-1.0, inc="s")
                P.act(spa, spa, AF.Ln, bias=1.0, inc="s")
                s_exp[par] = P.c("s")
                P.vop("tensor_scalar", spb, bk[:, 0:512], dtb[:, dd:dd + 1],
                      0.0, ALU.add, ALU.max, waits=P.after("s"), inc="v")
                P.vop("tensor_tensor", dcol, spa, spb, ALU.add, inc="v")
            dD = ch(delta, dd, TC)
            P.vop("tensor_tensor", ch(dub, dd, TC), dD, ch(ucs, dd, TC),
                  ALU.mult, waits=P.after("qy"), inc="v")
            P.vop("tensor_copy", ch(delb, dd, TC), dD, inc="v")
        P.vop("tensor_copy", drn[:, 0:1], dub[:, 2 * TC - 1:2 * TC], inc="v")
        P.vop("tensor_copy", drn[:, 1:2], delb[:, 2 * TC - 1:2 * TC], inc="v")
        v_dt = P.c("v")

        if tc == 0 or tc == NTC // 2:
            P.vop("memset", hst, 0.0, inc="v")
            pend_g = None

        for g in range(32):
            if g == 16:
                w4_flush()
                w4_dma(2)
            dd, gg = g // 16, g % 16
            Gb, cc_ = gg // 4, gg % 4
            tp = (32 * Gb, 0)
            selp = sel[32 * Gb:32 * Gb + 32, cc_ * 128:(cc_ + 1) * 128]
            at = a_t2[g % 2]
            for su in range(4):
                pd = subidx % 3
                pu = subidx % 4
                subidx += 1
                rng = slice(su * 512, (su + 1) * 512)
                wdb_ = [] if s_exp[pd] is None else [("s", s_exp[pd])]
                if su == 0:
                    wdb_.append(("v", v_dt))
                    if v_scan[g % 2] is not None:
                        wdb_.append(("v", v_scan[g % 2]))
                P.mm(db_banks[pd][:, 0:512], selp,
                     ch(delb, dd, TC)[32 * Gb:32 * Gb + 32, rng], True, True,
                     tp=tp, waits=wdb_, inc="t")
                wdu_ = [] if v_bt[pu] is None else [("v", v_bt[pu])]
                P.mm(du_banks[pu][:, 0:512], selp,
                     ch(dub, dd, TC)[32 * Gb:32 * Gb + 32, rng], True, True,
                     tp=tp, waits=wdu_, inc="t")
                P.act(at[:, rng], db_banks[pd][:, 0:512], AF.Exp,
                      scale=aneg[:, g:g + 1], waits=P.after("t"), inc="s")
                s_exp[pd] = P.c("s")
                P.vop("tensor_tensor", b_t[:, rng], du_banks[pu][:, 0:512],
                      brep[:, rng], ALU.mult, waits=P.after("t"), inc="v")
                v_bt[pu] = P.c("v")
            if pend_ysel is not None:
                flush_tc = pend_ysel[1]
                emit_ysel(*pend_ysel)
                pend_ysel = None
                if g == 0 and tc > 0:
                    collective("AllGather", y_agi[tc - 1],
                               y_ago[tc - 1], waits=P.after("qy"))
            P.vop("tensor_tensor_scan", h_t2[g % 2][:, 0:TC + 1],
                  at[:, 0:TC + 1], b_t[:, 0:TC + 1], hst[:, g:g + 1],
                  ALU.mult, ALU.add, waits=[("s", P.c("s"))], inc="v")
            v_scan[g % 2] = P.c("v")
            P.vop("memset", spc[:, 0:1], 0.0, inc="v")
            if pend_g is not None:
                P.vop("tensor_copy", hst[:, pend_g:pend_g + 1],
                      h_t2[pend_g % 2][:, TC - 1:TC], inc="v")
            pend_g = g
            P.vop("tensor_tensor", ypb2[g % 2], h_t2[g % 2][:, 0:TC], crep,
                  ALU.mult, inc="v")
            P.vop("tensor_copy", spc[:, 1:2], ypb2[g % 2][:, TC - 1:TC],
                  inc="v")
            v_ym = P.c("v")
            pend_ysel = (g, tc, v_ym)

    if pend_ysel is not None:
        emit_ysel(*pend_ysel)
        pend_ysel = None
    w4_flush()
    collective("AllGather", y_agi[NTC - 1],
               y_ago[NTC - 1], waits=P.after("qy"))
    CC_Y = P.c("cc")
    wall_all()

    arf.reset(0)
    y4d = [arf.take(16 * T4), arf.take(16 * T4)]
    res4d = [arf.take(16 * T4), arf.take(16 * T4)]
    o4d = [arf.take(8 * T4), arf.take(8 * T4)]
    xn4 = arf.take(8 * T4)
    sq4 = arf.take(T4)
    st4_a = arf.take(4 * T4)
    st4 = st4_a[0:1, :]
    arb.reset(0)
    uc4 = arb.take(16 * T4)
    ytbd = [arb.take(16 * T4), arb.take(16 * T4)]
    ob4 = arb.take(8 * T4)
    h1b = arb.take(8 * T4)

    def srow4(slot, n=T4):
        return st4[:, slot * T4:slot * T4 + n]

    NT4 = TPC // T4
    ld1 = {}
    ld2 = {}
    ld3 = {}
    qg_out = {}
    v_zdone = {}
    v_fin = {}

    def emit_L1(it):
        p = it % 2
        t0 = it * T4

        def ycopy(e, t0=t0):
            j = _core_j(e)
            return e.dma_start(out=y_loc[:, :, t0:t0 + T4],
                               in_=y_ago[ds(j, 1), :, :, t0:t0 + T4])
        P.dyndma("sync", ycopy, waits=([("cc", CC_Y)] if it == 0 else ()))
        cpy = P.c("qy")
        for c in range(16):
            w = [("qy", cpy)] if c == 0 else ()
            if c == 0 and it >= 2:
                w = w + [("qg", qg_out[it - 2])]
            P.dma("sync", ch(y4d[p], c, T4),
                  y_loc[c // 2, 128 * (c % 2):128 * (c % 2) + 128,
                        t0:t0 + T4], waits=w)
        for c in range(16):
            P.dma("sync", ch(res4d[p], c, T4), res_dr_v[c, :, t0:t0 + T4])
            P.dma("sync", ch(uc4, c, T4),
                  uc_agi_v[it // 2, c, :, (it % 2) * T4:(it % 2) * T4 + T4])
        ld1[it] = P.c("qy")

    def emit_L2(it, wv):
        t0 = it * T4
        for c in range(8):
            P.dma("sync", ch(xn4, c, T4), xn_dr_v[c, :, t0:t0 + T4],
                  waits=([("v", wv)] if (c == 0 and wv is not None) else ()))
        ld2[it] = P.c("qy")

    emit_L1(0)
    emit_L2(0, None)

    for it in range(NT4):
        p = it % 2
        t0 = it * T4
        bb = 0 if p == 0 else 4
        z4 = y4d[p]
        x4v = lambda c: res4d[p][:, c * T4:(c + 1) * T4]
        mo4v = lambda c: res4d[p][:, 8 * T4 + c * T4:8 * T4 + (c + 1) * T4]

        P.act(res4d[p], res4d[p], AF.Silu, waits=[("qy", ld1[it])], inc="s")
        s_sil = P.c("s")
        for c in range(16):
            P.vop("scalar_tensor_tensor", ch(y4d[p], c, T4), ch(uc4, c, T4),
                  dpc[:, c:c + 1], ch(y4d[p], c, T4), ALU.mult, ALU.add,
                  waits=([("qy", ld1[it])] if c == 0 else ()), inc="v")
        P.vop("tensor_tensor", ytbd[p], y4d[p], res4d[p], ALU.mult,
              waits=[("s", s_sil)], inc="v")
        P.vop("tensor_copy", drn[:, 0:1],
              ytbd[p][:, 16 * T4 - 1:16 * T4], inc="v")
        v_ytb = P.c("v")

        for c in range(8):
            P.dma("sync", x4v(c), x_fm_v[c, :, 3 + t0:3 + t0 + T4],
                  waits=([("v", v_ytb)] if c == 0 else ()))
        for c in range(8):
            P.dma("sync", mo4v(c), moe_dr_v[c, :, t0:t0 + T4])
        ld3[it] = P.c("qy")
        if it + 1 < NT4:
            emit_L1(it + 1)

        ope_v = {}
        for m in range(8):
            bk = pb[bb + (m % 2)]
            if m == 0:
                w0 = [("v", max(v_ytb, v_zdone.get(it - 2, 0)))]
            elif m >= 2:
                w0 = [("v", ope_v[m - 2])]
            else:
                w0 = ()
            for k in range(16):
                P.mm(bk[:, 0:T4],
                     wopb[:, k * 1024 + m * 128:k * 1024 + (m + 1) * 128],
                     ch(ytbd[p], k, T4), start=(k == 0), stop=(k == 15),
                     waits=(w0 if k == 0 else ()),
                     inc=("t" if k == 15 else None))
            t_op = P.c("t")
            P.vop("tensor_tensor", ch(o4d[p], m, T4), bk[:, 0:T4],
                  ch(xn4, m, T4), ALU.add,
                  waits=[("t", t_op)] + ([("qy", ld2[it])] if m == 0 else []),
                  inc="v")
            P.vop("tensor_copy", ch(ob4, m, T4), ch(o4d[p], m, T4), inc="v")
            ope_v[m] = P.c("v")
        P.vop("tensor_copy", drn[:, 0:1], ob4[:, 8 * T4 - 1:8 * T4], inc="v")
        v_ob_all = P.c("v")
        if it + 1 < NT4:
            emit_L2(it + 1, ope_v[7])

        s_gelu = {}
        for m in range(8):
            bk = pb[bb + (m % 2)]
            if m == 0:
                w0 = [("v", v_ob_all)]
            elif m >= 2:
                w0 = [("s", s_gelu[m - 2])]
            else:
                w0 = ()
            for k in range(8):
                P.mm(bk[:, 0:T4],
                     wf1b[:, k * 1024 + m * 128:k * 1024 + (m + 1) * 128],
                     ch(ob4, k, T4), start=(k == 0), stop=(k == 7),
                     waits=(w0 if k == 0 else ()),
                     inc=("t" if k == 7 else None))
            P.act(ch(h1b, m, T4), bk[:, 0:T4], AF.Gelu, bias=b1c[:, m:m + 1],
                  waits=[("t", P.c("t"))], inc="s")
            s_gelu[m] = P.c("s")
        s_gelu_all = P.c("s")

        zadd_v = {}
        for m in range(8):
            bk = pb[bb + (m % 2)]
            if m == 0:
                w0 = [("s", s_gelu_all)]
            elif m >= 2:
                w0 = [("v", zadd_v[m - 2])]
            else:
                w0 = ()
            for k in range(8):
                P.mm(bk[:, 0:T4],
                     wf2b[:, k * 1024 + m * 128:k * 1024 + (m + 1) * 128],
                     ch(h1b, k, T4), start=(k == 0), stop=(k == 7),
                     waits=(w0 if k == 0 else ()),
                     inc=("t" if k == 7 else None))
            P.vop("tensor_scalar_add", ch(z4, m, T4), bk[:, 0:T4],
                  b2c[:, m:m + 1], waits=[("t", P.c("t"))], inc="v")
            P.vop("tensor_tensor", ch(z4, m, T4), ch(z4, m, T4),
                  ch(o4d[p], m, T4), ALU.add, inc="v")
            zadd_v[m] = P.c("v")
        P.vop("tensor_copy", drn[:, 0:1], z4[:, 8 * T4 - 1:8 * T4], inc="v")
        v_zdone[it] = P.c("v")

        for c in range(8):
            P.mm(pb[2][0:1, 0:T4], onk, ch(z4, c, T4), start=(c == 0),
                 stop=(c == 7),
                 waits=([("v", v_zdone[it])] if c == 0 else ()),
                 inc=("t" if c == 7 else None))
        for c in range(8):
            P.act(sq4, ch(z4, c, T4), AF.Square, waits=P.after("t"), inc="s")
            P.mm(pb[3][0:1, 0:T4], onk, sq4, start=(c == 0), stop=(c == 7),
                 waits=P.after("s"), inc="t")
        mu4, v4 = srow4(0), srow4(1)
        r4, mr4 = srow4(2), srow4(3)
        P.vop("tensor_scalar_mul", mu4, pb[2][0:1, 0:T4], 1.0 / D,
              waits=P.after("t"), inc="v")
        P.vop("tensor_tensor", v4, mu4, mu4, ALU.mult, inc="v")
        P.vop("scalar_tensor_tensor", v4, pb[3][0:1, 0:T4], 1.0 / D, v4,
              ALU.mult, ALU.subtract, inc="v")
        P.act(r4, v4, AF.Ln, bias=EPS, waits=P.after("v"), inc="s")
        P.act(r4, r4, AF.Exp, scale=-0.5, inc="s")
        P.vop("tensor_tensor", mr4, mu4, r4, ALU.mult, waits=P.after("s"),
              inc="v")
        P.mm(pb[6][:, 0:T4], on1, r4, True, True, waits=P.after("v"), inc="t")
        P.mm(pb[7][:, 0:T4], on1, mr4, True, True, inc="t")
        t_bc = P.c("t")
        for c in range(8):
            w0 = []
            if c == 0:
                w0 = [("t", t_bc), ("qy", ld3[it])]
            P.vop("tensor_tensor", ch(z4, c, T4), ch(z4, c, T4),
                  pb[6][:, 0:T4], ALU.mult, waits=w0, inc="v")
            P.vop("tensor_tensor", ch(z4, c, T4), ch(z4, c, T4),
                  pb[7][:, 0:T4], ALU.subtract, inc="v")
            P.vop("tensor_scalar", ch(z4, c, T4), ch(z4, c, T4),
                  lnw[:, c:c + 1], lnb[:, c:c + 1], ALU.mult, ALU.add,
                  inc="v")
            P.vop("tensor_tensor", ch(z4, c, T4), ch(z4, c, T4), x4v(c),
                  ALU.add, inc="v")
            P.vop("tensor_tensor", ch(z4, c, T4), ch(z4, c, T4), mo4v(c),
                  ALU.add, inc="v")
            P.dma("gpsimd", out_fm_v[c, :, t0:t0 + T4], ch(z4, c, T4),
                  waits=P.after("v"))
        qg_out[it] = P.c("qg")
        v_fin[it] = P.c("v")

    for t, src in taps:
        P.dma("gpsimd", t, src, waits=P.after("qg", "v", "s", "t"))

    P.step("gpsimd", None, waits=P.after("qg", "qy"))
    P.step("sync", None, waits=P.after("qg"))

    with es, P.es:
        with nc.Block() as block:
            P.emit(block)

    return nc



def _prep_inputs(a):
    import ml_dtypes
    bf16 = ml_dtypes.bfloat16
    f32 = np.float32

    x = np.asarray(a["L0_attn_x"], f32)
    norm_w = np.asarray(a["L0_attn_norm_w"], f32)
    in_proj_w = np.asarray(a["L0_attn_in_proj_w"], f32)
    conv_w = np.asarray(a["L0_attn_conv_w"], f32)
    conv_b = np.asarray(a["L0_attn_conv_b"], f32)
    x_proj_w = np.asarray(a["L0_attn_x_proj_w"], f32)
    dt_w = np.asarray(a["L0_attn_dt_w"], f32)
    dt_b = np.asarray(a["L0_attn_dt_b"], f32)
    A_log = np.asarray(a["L0_attn_A_log"], f32)
    Dp = np.asarray(a["L0_attn_Dp"], f32)
    out_proj_w = np.asarray(a["L0_attn_out_proj_w"], f32)
    ff_w1 = np.asarray(a["L0_attn_ff_w1"], f32)
    ff_b1 = np.asarray(a["L0_attn_ff_b1"], f32)
    ff_w2 = np.asarray(a["L0_attn_ff_w2"], f32)
    ff_b2 = np.asarray(a["L0_attn_ff_b2"], f32)
    ff_ln_w = np.asarray(a["L0_attn_ff_ln_w"], f32)
    ff_ln_b = np.asarray(a["L0_attn_ff_ln_b"], f32)
    gate_w = np.asarray(a["L0_ffn_gate_w"], f32)
    w_up = np.asarray(a["L0_ffn_w_up"], f32)
    w_down = np.asarray(a["L0_ffn_w_down"], f32)
    sg_gate = np.asarray(a["L0_ffn_sg_gate"], f32)
    sg_up = np.asarray(a["L0_ffn_sg_up"], f32)
    sg_down = np.asarray(a["L0_ffn_sg_down"], f32)

    C = np.ascontiguousarray

    def col(v, nchunk):
        return C(v.reshape(nchunk, 128).T.astype(f32))

    shared = {
        "w_ip": C(in_proj_w.T),
        "nrm_col": col(norm_w, 8),
        "cw_col": C(conv_w.reshape(16, 128, 4).transpose(1, 0, 2).reshape(128, 64)),
        "cb_col": col(conv_b, 16),
        "dp_col": col(Dp, 16),
        "w_op": C(out_proj_w.T),
        "w_f1": C(ff_w1.T), "b1_col": col(ff_b1, 8),
        "w_f2": C(ff_w2.T), "b2_col": col(ff_b2, 8),
        "lnw_col": col(ff_ln_w, 8), "lnb_col": col(ff_ln_b, 8),
        "w_gate": C(gate_w.T.reshape(8, 128, 8).transpose(1, 0, 2).reshape(128, 64)),
        "ones1": np.ones((1, 128), f32),
        "onesk": np.ones((128, 1), f32),
        "ident": np.eye(128, dtype=f32),
    }
    xp_wide = np.concatenate([x_proj_w[0:64],
                              np.tile(x_proj_w[64:80], (8, 1)),
                              np.tile(x_proj_w[80:96], (8, 1))], axis=0)
    shared["w_xp"] = C(xp_wide.T)
    base = np.zeros((32, 512), f32)
    for c in range(4):
        for p in range(128):
            base[8 * c + p // 16, c * 128 + p] = 1.0
    shared["sel4"] = C(np.tile(base, (4, 1)).astype(bf16))
    ys = np.zeros((128, 8), f32)
    for p in range(128):
        ys[p, p // 16] = 1.0
    shared["ysel_c"] = C(ys.astype(bf16))

    parts = []
    for e in range(NEXP):
        parts.append(C(w_up[e, :FFN, :].T).ravel())
        parts.append(C(w_up[e, FFN:, :].T).ravel())
        parts.append(C(w_down[e].T).ravel())
    parts.append(C(sg_gate.T).ravel())
    parts.append(C(sg_up.T).ravel())
    parts.append(C(sg_down.T).ravel())
    moe_flat = np.concatenate(parts).astype(f32)
    assert moe_flat.size == MOE_TOT

    in_maps = []
    for j in range(NC):
        b, p0 = j // 4, (j % 4) * TPC
        xs = np.zeros((D, TPC + 3), f32)
        if p0 == 0:
            xs[:, 3:] = x[b, 0:TPC].T
        else:
            xs[:, :] = x[b, p0 - 3:p0 + TPC].T
        sl = slice(DSL * j, DSL * (j + 1))
        al = A_log[sl].reshape(32, 8, 16).transpose(1, 2, 0).reshape(128, 32)
        m = dict(shared)
        m["x_fm"] = xs
        m["w_dt"] = C(dt_w[sl].T)
        m["dtb_col"] = col(dt_b[sl], 2)
        m["alog_pk"] = C(al)
        m["w_moe_sl"] = C(moe_flat[j * MOE_SL:(j + 1) * MOE_SL]
                          .reshape(MOE_SL // 2048, 2048))
        in_maps.append(m)
    return in_maps


_NC_CACHE = {}


def kernel(**inputs):
    in_maps = _prep_inputs(inputs)
    if "prog" not in _NC_CACHE:
        _NC_CACHE["prog"] = build_program(debug=False)
    nc = _NC_CACHE["prog"]
    res = run_bass_kernel_spmd(nc, in_maps, core_ids=list(range(NC)))
    out = np.empty((B, L, D), np.float32)
    for j in range(NC):
        b, p0 = j // 4, (j % 4) * TPC
        out[b, p0:p0 + TPC, :] = res.results[j]["out_fm"].T
    return out


# version 16
# speedup vs baseline: 1.0523x; 1.0523x over previous
import contextlib
import numpy as np

import concourse.bass as bass
import concourse.mybir as mybir
from concourse.bass_utils import run_bass_kernel_spmd

F32 = mybir.dt.float32
BF16 = mybir.dt.bfloat16
AF = mybir.ActivationFunctionType
ALU = mybir.AluOpType
AX = mybir.AxisListType
ds = bass.ds

NC = 8
B, L, D, DI, NST, DTR = 2, 8192, 1024, 2048, 16, 64
TOK = B * L
TPC = TOK // NC
DSL = DI // NC
T1 = 512
W1 = T1 + 3
TM = 512
T4 = 256
NEXP = 8
FFN = 1536
EPS = 1e-5
BIG = 1.0e30
TC = 2048
NTC = TOK // TC

EXP_W = 3 * 1024 * FFN
MOE_TOT = (NEXP + 1) * EXP_W
MOE_SL = MOE_TOT // NC
WG_SZ = 1024 * FFN

AR_F = 24000
AR_B = 54000

ENGS = ("sync", "scalar", "vector", "tensor", "gpsimd")


class Prog:
    def __init__(self, nc):
        self.nc = nc
        self.es = contextlib.ExitStack()
        self.sems = {}
        self.cnt = {}
        self.steps = {e: [] for e in ENGS}

    def sem(self, name):
        if name not in self.sems:
            self.sems[name] = self.es.enter_context(self.nc.semaphore(name))
            self.cnt[name] = 0
        return self.sems[name]

    def step(self, eng, fn=None, waits=(), inc=None, inc_by=1):
        if inc is not None:
            self.sem(inc)
            self.cnt[inc] += inc_by
        for s, _ in waits:
            self.sem(s)
        self.steps[eng].append((fn, tuple(waits), inc, inc_by))

    def c(self, name):
        self.sem(name)
        return self.cnt[name]

    def after(self, *names):
        return [(n, self.c(n)) for n in names]

    def dma(self, eng, out, in_, waits=(), accum=None):
        q = "qy" if eng == "sync" else "qg"

        def f(e, out=out, in_=in_, accum=accum):
            if accum is None:
                return e.dma_start(out=out, in_=in_)
            return e.dma_start(out=out, in_=in_, accum_op=accum)
        self.step(eng, f, waits=waits, inc=q, inc_by=16)

    def dyndma(self, eng, fn, waits=()):
        q = "qy" if eng == "sync" else "qg"
        self.step(eng, fn, waits=waits, inc=q, inc_by=16)

    def mm(self, out, lhsT, rhs, start, stop, tp=None, waits=(), inc=None):
        def f(e, out=out, lhsT=lhsT, rhs=rhs, start=start, stop=stop, tp=tp):
            if tp is None:
                return e.matmul(out, lhsT, rhs, start=start, stop=stop)
            return e.matmul(out, lhsT, rhs, start=start, stop=stop,
                            tile_position=tp)
        self.step("tensor", f, waits=waits, inc=inc)

    def trans(self, out, in_, ident, waits=(), inc=None):
        def f(e, out=out, in_=in_, ident=ident):
            return e.transpose(out, in_, ident)
        self.step("tensor", f, waits=waits, inc=inc)

    def act(self, out, in_, func, bias=0.0, scale=1.0, waits=(), inc=None):
        def f(e, out=out, in_=in_, func=func, bias=bias, scale=scale):
            return e.activation(out, in_, func, bias=bias, scale=scale)
        self.step("scalar", f, waits=waits, inc=inc)

    def scopy(self, out, in_, waits=(), inc=None):
        def f(e, out=out, in_=in_):
            return e.copy(out, in_)
        self.step("scalar", f, waits=waits, inc=inc)

    def vop(self, name, *args, waits=(), inc=None):
        def f(e, name=name, args=args):
            return getattr(e, name)(*args)
        self.step("vector", f, waits=waits, inc=inc)

    def vwait(self, waits):
        self.step("vector", None, waits=waits)

    def emit(self, block):
        def make(eng_name):
            def body(eng):
                for fn, waits, inc, inc_by in self.steps[eng_name]:
                    for s, v in waits:
                        eng.wait_ge(self.sems[s], v)
                    ins = fn(eng) if fn is not None else None
                    if inc is not None:
                        if ins is None:
                            ins = eng.nop()
                        ins.then_inc(self.sems[inc], inc_by)
            return body
        block.sync(make("sync"))
        block.scalar(make("scalar"))
        block.vector(make("vector"))
        block.tensor(make("tensor"))
        block.gpsimd(make("gpsimd"))


class Bump:
    def __init__(self, ap, elems, name):
        self.ap, self.n, self.off, self.name = ap, elems, 0, name

    def take(self, elems):
        elems = int(elems)
        off = (self.off + 15) // 16 * 16
        v = self.ap[:, off:off + elems]
        self.off = off + elems
        assert self.off <= self.n, f"{self.name} overflow {self.off}>{self.n}"
        return v

    def mark(self):
        return self.off

    def reset(self, m=0):
        self.off = m


def ch(ap_, c, w):
    return ap_[:, c * w:(c + 1) * w]


def _core_offs(e):
    if not hasattr(e, "_mk_offs"):
        j = e.partition_id()
        jD0 = e.snap(j * DSL, min_val=0, max_val=(NC - 1) * DSL)
        jT0 = e.snap(j * TPC, min_val=0, max_val=(NC - 1) * TPC)
        e._mk_offs = (jD0, jT0)
    return e._mk_offs


def _core_j(e):
    if not hasattr(e, "_mk_j"):
        e._mk_j = e.snap(e.partition_id(), min_val=0, max_val=NC - 1)
    return e._mk_j


def build_program(debug=False):
    nc = bass.Bass("TRN2", target_bir_lowering=False)

    def din(name, shape, dt=F32):
        return nc.dram_tensor(name, shape, dt, kind="ExternalInput").ap()

    def dint(name, shape, dt=F32):
        return nc.dram_tensor(name, shape, dt).ap()

    x_fm = din("x_fm", [D, TPC + 3])
    w_ip = din("w_ip", [D, 2 * DI])
    nrm_col = din("nrm_col", [128, D // 128])
    cw_col = din("cw_col", [128, (DI // 128) * 4])
    cb_col = din("cb_col", [128, DI // 128])
    w_xp = din("w_xp", [DI, 320])
    w_dt = din("w_dt", [DTR, DSL])
    dtb_col = din("dtb_col", [128, DSL // 128])
    alog_pk = din("alog_pk", [128, DSL // 8])
    dp_col = din("dp_col", [128, DI // 128])
    w_op = din("w_op", [DI, D])
    w_f1 = din("w_f1", [D, D])
    b1_col = din("b1_col", [128, D // 128])
    w_f2 = din("w_f2", [D, D])
    b2_col = din("b2_col", [128, D // 128])
    lnw_col = din("lnw_col", [128, D // 128])
    lnb_col = din("lnb_col", [128, D // 128])
    w_gate = din("w_gate", [128, (D // 128) * 8])
    w_moe_sl = din("w_moe_sl", [MOE_SL // 2048, 2048], F32)
    sel4 = din("sel4", [128, 512], BF16)
    ysel_c = din("ysel_c", [128, 8], BF16)
    ones1 = din("ones1", [1, 128], F32)
    onesk = din("onesk", [128, 1], F32)
    ident = din("ident", [128, 128], F32)

    out_fm = nc.dram_tensor("out_fm", [D, TPC], F32, kind="ExternalOutput").ap()

    uc_agi = dint("uc_agi", [4, DI, T1], BF16)
    uc_ago = dint("uc_ago", [4, NC, DI, T1], BF16)
    bc_agi = dint("bc_agi", [4, 256, T1], F32)
    bc_ago = dint("bc_ago", [4, NC, 256, T1], F32)
    dtr_agi = dint("dtr_agi", [4, DTR, T1], BF16)
    dtr_ago = dint("dtr_ago", [4, NC, DTR, T1], BF16)
    wm_agi = dint("wm_agi", [MOE_SL // 2048, 2048], BF16)
    wm_ago = dint("wm_ago", [NC, MOE_SL], BF16)
    y_agi = dint("y_agi", [NTC, DSL, TC], F32)
    y_ago = dint("y_ago", [NTC, NC, DSL, TC], F32)
    uc_loc = dint("uc_loc", [4, NC, DSL, T1], BF16)
    res_dr = dint("res_dr", [DI, TPC], F32)
    xn_dr = dint("xn_dr", [D, TPC], F32)
    xnb_dr = dint("xnb_dr", [D, TPC], BF16)
    comb_dr = dint("comb_dr", [NEXP, TPC], F32)
    moe_dr = dint("moe_dr", [D, TPC], F32)
    y_loc = dint("y_loc", [NC, DSL, TPC], F32)

    wm_flat = wm_ago.rearrange("a b -> (a b)")

    def moe_w(off, k, m):
        return wm_flat[off:off + k * m].rearrange("(k m) -> k m", m=m)

    es = contextlib.ExitStack()

    def sb(name, shape, dt):
        return es.enter_context(nc.sbuf_tensor(name, shape, dt)).ap()

    nrm = sb("nrm", [128, D // 128], F32)
    cw = sb("cw", [128, (DI // 128) * 4], F32)
    cbx = sb("cbx", [128, DI // 128], F32)
    dtb = sb("dtb", [128, DSL // 128], F32)
    aneg = sb("aneg", [128, DSL // 8], F32)
    dpc = sb("dpc", [128, DI // 128], F32)
    b1c = sb("b1c", [128, D // 128], F32)
    b2c = sb("b2c", [128, D // 128], F32)
    lnw = sb("lnw", [128, D // 128], F32)
    lnb = sb("lnb", [128, D // 128], F32)
    sel = sb("sel", [128, 512], BF16)
    ysl = sb("ysl", [128, 8], BF16)
    on1 = sb("on1", [1, 128], F32)
    onk = sb("onk", [128, 1], F32)
    idn = sb("idn", [128, 128], F32)
    wgt8 = sb("wgt8", [128, (D // 128) * 8], F32)
    wdtb = sb("wdtb", [DTR, DSL], BF16)
    hst = sb("hst", [128, DSL // 8], F32)
    drn = sb("drn", [128, 2], F32)

    arf_t = sb("arf", [128, AR_F], F32)
    arb_t = sb("arb", [128, AR_B], BF16)
    arf = Bump(arf_t, AR_F, "arf")
    arb = Bump(arb_t, AR_B, "arb")

    cst_eps = sb("cst_eps", [128, 1], F32)
    nc.const_aps.register(F32, EPS, cst_eps)

    pb = [es.enter_context(nc.psum_tensor(f"pb{i}", [128, 512], F32)).ap()
          for i in range(8)]

    P = Prog(nc)

    def wall_all():
        w = P.after("qy", "qg", "s", "v", "t")
        for engn in ENGS:
            P.step(engn, None, waits=w)

    taps = []
    if debug:
        for nm, src, shape, dt in (
                ("xn", xn_dr, [D, TPC], F32),
                ("uc", uc_agi, [4, DI, T1], BF16),
                ("bc", bc_agi, [4, 256, T1], F32),
                ("dtr", dtr_agi, [4, DTR, T1], BF16),
                ("res", res_dr, [DI, TPC], F32),
                ("comb", comb_dr, [NEXP, TPC], F32),
                ("y", y_agi, [NTC, DSL, TC], F32),
                ("moe", moe_dr, [D, TPC], F32),
                ("xnb", xnb_dr, [D, TPC], BF16)):
            t = nc.dram_tensor("tap_" + nm, shape, dt, kind="ExternalOutput").ap()
            taps.append((t, src))

    x_fm_v = x_fm.rearrange("(c p) t -> c p t", p=128)
    uc_agi_v = uc_agi.rearrange("q (c p) t -> q c p t", p=128)
    res_dr_v = res_dr.rearrange("(c p) t -> c p t", p=128)
    xn_dr_v = xn_dr.rearrange("(c p) t -> c p t", p=128)
    xnb_dr_v = xnb_dr.rearrange("(c p) t -> c p t", p=128)
    moe_dr_v = moe_dr.rearrange("(c p) t -> c p t", p=128)
    out_fm_v = out_fm.rearrange("(c p) t -> c p t", p=128)

    for o_, i_ in ((nrm, nrm_col), (cw, cw_col), (cbx, cb_col), (dtb, dtb_col),
                   (dpc, dp_col), (b1c, b1_col), (b2c, b2_col), (lnw, lnw_col),
                   (lnb, lnb_col), (sel, sel4), (ysl, ysel_c), (on1, ones1),
                   (onk, onesk), (idn, ident), (aneg, alog_pk), (wgt8, w_gate)):
        P.dma("sync", o_, i_)
    n_const = P.c("qy")

    P.vop("memset", cst_eps, EPS, inc="v")
    P.act(aneg, aneg, AF.Exp, waits=[("qy", n_const)], inc="s")
    P.vop("tensor_scalar_mul", aneg, aneg, -1.0, waits=P.after("s"), inc="v")

    wipb = arb.take(8 * 2 * DI)
    wxpb = arb.take(16 * 320)
    MB_P1 = arb.mark()
    wmb = arb.take(2048)
    wstg = arf.take(2048)

    w_ip_v = w_ip.rearrange("(c p) m -> c p m", p=128)
    w_xp_v = w_xp.rearrange("(c p) m -> c p m", p=128)

    for c in range(8):
        for h in range(2):
            P.dma("sync", wstg[:, 0:2048], w_ip_v[c, :, h * 2048:(h + 1) * 2048],
                  waits=P.after("v"))
            P.vop("tensor_scalar_mul",
                  wipb[:, c * 4096 + h * 2048:c * 4096 + (h + 1) * 2048],
                  wstg[:, 0:2048], nrm[:, c:c + 1],
                  waits=P.after("qy"), inc="v")
    for c in range(16):
        P.dma("sync", wstg[:, 0:320], w_xp_v[c], waits=P.after("s", "v"))
        P.scopy(ch(wxpb, c, 320), wstg[:, 0:320], waits=P.after("qy"), inc="s")
    P.dma("sync", wstg[0:DTR, 0:DSL], w_dt, waits=P.after("s"))
    P.scopy(wdtb, wstg[0:DTR, 0:DSL], waits=P.after("qy"), inc="s")

    n_rows = MOE_SL // 2048
    for r0 in range(0, n_rows, 128):
        nr = min(128, n_rows - r0)
        P.dma("sync", wstg[0:nr, 0:2048], w_moe_sl[r0:r0 + nr],
              waits=P.after("qg", "s"))
        P.scopy(wmb[0:nr, :], wstg[0:nr, 0:2048], waits=P.after("qy"), inc="s")
        P.dma("gpsimd", wm_agi[r0:r0 + nr], wmb[0:nr, :], waits=P.after("s"))

    def collective(kind, ins, outs, waits=()):
        def f(e, kind=kind, ins=ins, outs=outs):
            return e.collective_compute(
                kind, ALU.bypass, replica_groups=[list(range(NC))],
                ins=[ins.opt()], outs=[outs.opt()])
        P.step("gpsimd", f, waits=waits, inc="cc")

    collective("AllGather", wm_agi, wm_ago, waits=P.after("qg"))
    CC_WMOE = P.c("cc")
    wall_all()

    arf.reset(0)
    logsb_a = arf.take(TPC)
    logsb = logsb_a[0:8, :]
    MF_P1 = arf.mark()
    xt = arf.take(8 * W1)
    xnf = arf.take(8 * W1)
    ut = arf.take(16 * W1)
    convo = arf.take(T1)
    resst = arf.take(2 * W1)
    statr_a = arf.take(4 * W1)
    statr = statr_a[0:1, :]
    arb.reset(MB_P1)
    xn2b = arb.take(8 * W1)
    ucb = arb.take(16 * T1)
    xnbst = ucb[:, 0:8 * T1]
    dtrst = arb.take(T1)

    S_MU, S_RSTD, S_MR, S_V = 0, 1, 2, 3

    def srow(slot, off, n):
        return statr[:, slot * W1 + off:slot * W1 + off + n]

    for it in range(TPC // T1):
        t0 = it * T1
        halo = (it == 0)
        xo = 0 if halo else 3
        nnew = W1 if halo else T1
        src_off = t0 if halo else t0 + 3
        if not halo:
            uv = ut.rearrange("p (c w) -> p c w", w=W1)
            P.vop("tensor_copy", uv[:, :, 0:3], uv[:, :, T1:T1 + 3],
                  waits=P.after("t", "s"), inc="v")
        for c in range(8):
            P.dma("sync", ch(xt, c, W1)[:, xo:xo + nnew],
                  x_fm_v[c, :, src_off:src_off + nnew],
                  waits=(P.after("v", "s", "t") if c == 0 else ()))
        ldx = P.c("qy")

        subs = [(0, 512), (3, 512)] if halo else [(3, 512)]
        for off, n in subs:
            for c in range(8):
                P.mm(pb[7][0:1, 0:n], onk, ch(xt, c, W1)[:, off:off + n],
                     start=(c == 0), stop=(c == 7),
                     waits=(P.after("qy", "v", "s") if c == 0 else ()),
                     inc=("t" if c == 7 else None))
            for c in range(8):
                P.act(convo[:, 0:n], ch(xt, c, W1)[:, off:off + n], AF.Square,
                      waits=P.after("t"), inc="s")
                P.mm(pb[6][0:1, 0:n], onk, convo[:, 0:n],
                     start=(c == 0), stop=(c == 7), waits=P.after("s"),
                     inc="t")
            mu, rstd = srow(S_MU, off, n), srow(S_RSTD, off, n)
            mr, vv = srow(S_MR, off, n), srow(S_V, off, n)
            P.vop("tensor_scalar_mul", mu, pb[7][0:1, 0:n], 1.0 / D,
                  waits=P.after("t"), inc="v")
            P.vop("tensor_tensor", vv, mu, mu, ALU.mult, inc="v")
            P.vop("scalar_tensor_tensor", vv, pb[6][0:1, 0:n], 1.0 / D, vv,
                  ALU.mult, ALU.subtract, inc="v")
            P.act(rstd, vv, AF.Ln, bias=EPS, waits=P.after("v"), inc="s")
            P.act(rstd, rstd, AF.Exp, scale=-0.5, inc="s")
            P.vop("tensor_tensor", mr, mu, rstd, ALU.mult,
                  waits=P.after("s"), inc="v")
            P.mm(pb[5][:, 0:n], on1, rstd, True, True, waits=P.after("v"), inc="t")
            P.mm(pb[4][:, 0:n], on1, mr, True, True, inc="t")
            for c in range(8):
                P.vop("tensor_tensor", ch(xnf, c, W1)[:, off:off + n],
                      ch(xt, c, W1)[:, off:off + n], pb[5][:, 0:n], ALU.mult,
                      waits=(P.after("t", "qy") if c == 0 else ()), inc="v")
                P.vop("tensor_tensor", ch(xnf, c, W1)[:, off:off + n],
                      ch(xnf, c, W1)[:, off:off + n], pb[4][:, 0:n],
                      ALU.subtract, inc="v")
            for c in range(8):
                P.act(convo[:, 0:n], ch(xnf, c, W1)[:, off:off + n], AF.Square,
                      waits=P.after("v", "t"), inc="s")
                P.mm(pb[6][0:1, 0:n], onk, convo[:, 0:n],
                     start=(c == 0), stop=(c == 7), waits=P.after("s"),
                     inc="t")
            r2 = srow(S_RSTD, off, n)
            P.act(r2, pb[6][0:1, 0:n], AF.Ln, bias=EPS, scale=1.0 / D,
                  waits=P.after("t"), inc="s")
            P.act(r2, r2, AF.Exp, scale=-0.5, inc="s")
            P.mm(pb[5][:, 0:n], on1, r2, True, True,
                 waits=P.after("s", "v"), inc="t")
            for c in range(8):
                P.vop("tensor_tensor", ch(xn2b, c, W1)[:, off:off + n],
                      ch(xnf, c, W1)[:, off:off + n], pb[5][:, 0:n], ALU.mult,
                      waits=(P.after("t") if c == 0 else ()), inc="v")
            P.vop("tensor_copy", drn[:, 0:1],
                  ch(xn2b, 7, W1)[:, off + n - 1:off + n], inc="v")

        P.scopy(xnbst.rearrange("p (c w) -> p c w", w=T1),
                xnf.rearrange("p (c w) -> p c w", w=W1)[:, :, 3:3 + T1],
                waits=P.after("v", "qy"), inc="s")
        for c in range(8):
            P.dma("sync", xn_dr_v[c, :, t0:t0 + T1],
                  ch(xnf, c, W1)[:, 3:3 + T1],
                  waits=(P.after("v") if c == 0 else ()))
            P.dma("sync", xnb_dr_v[c, :, t0:t0 + T1], ch(xnbst, c, T1),
                  waits=(P.after("s") if c == 0 else ()))
        for c in range(8):
            P.mm(pb[7][0:8, 0:T1], wgt8[:, c * 8:(c + 1) * 8],
                 ch(xnf, c, W1)[:, 3:3 + T1], start=(c == 0), stop=(c == 7),
                 waits=(P.after("v") if c == 0 else ()),
                 inc=("t" if c == 7 else None))
        P.vop("tensor_copy", logsb[:, t0:t0 + T1], pb[7][0:8, 0:T1],
              waits=P.after("t"), inc="v")

        for m in range(32):
            bk = pb[5 + (m % 2)]
            for off, n in subs:
                for k in range(8):
                    P.mm(bk[:, 0:n],
                         wipb[:, k * 4096 + m * 128:k * 4096 + (m + 1) * 128],
                         ch(xn2b, k, W1)[:, off:off + n],
                         start=(k == 0), stop=(k == 7),
                         waits=(P.after("v", "s") if k == 0 else ()),
                         inc=("t" if k == 7 else None))
                if m < 16:
                    P.vop("tensor_copy", ch(ut, m, W1)[:, off:off + n],
                          bk[:, 0:n], waits=P.after("t"), inc="v")
                else:
                    half = 0 if (m % 2 == 0) else W1
                    P.scopy(resst[:, half + off:half + off + n],
                            bk[:, 0:n], waits=P.after("t", "qy"), inc="s")
            if m >= 16:
                half = 0 if (m % 2 == 0) else W1
                P.dma("sync", res_dr_v[m - 16, :, t0:t0 + T1],
                      resst[:, half + 3:half + 3 + T1], waits=P.after("s"))

        for c in range(16):
            P.vop("tensor_scalar_mul", convo, ch(ut, c, W1)[:, 0:T1],
                  cw[:, 4 * c:4 * c + 1], waits=P.after("v", "s"), inc="v")
            for k in (1, 2, 3):
                P.vop("scalar_tensor_tensor", convo,
                      ch(ut, c, W1)[:, k:k + T1], cw[:, 4 * c + k:4 * c + k + 1],
                      convo, ALU.mult, ALU.add, inc="v")
            P.act(ch(ucb, c, T1), convo, AF.Silu, bias=cbx[:, c:c + 1],
                  waits=P.after("v", "qy"), inc="s")
            P.dma("sync", uc_agi_v[it, c], ch(ucb, c, T1),
                  waits=P.after("s"))

        for mi, (mo, msz, bank) in enumerate(((0, 64, 5), (64, 128, 6),
                                              (192, 128, 5))):
            bk = pb[bank]
            for k in range(16):
                P.mm(bk[0:msz, 0:T1], wxpb[:, k * 320 + mo:k * 320 + mo + msz],
                     ch(ucb, k, T1), start=(k == 0), stop=(k == 15),
                     waits=(P.after("s", "v") if k == 0 else ()),
                     inc=("t" if k == 15 else None))
            if mi == 0:
                P.scopy(dtrst[0:DTR, 0:T1], bk[0:64, 0:T1],
                        waits=P.after("t", "qy"), inc="s")
                P.dma("sync", dtr_agi[it], dtrst[0:DTR, 0:T1],
                      waits=P.after("s"))
            else:
                half = 0 if mi == 1 else W1
                P.vop("tensor_copy", resst[:, half:half + T1], bk[0:128, 0:T1],
                      waits=P.after("t", "qy"), inc="v")
                P.dma("sync", bc_agi[it, (mi - 1) * 128:mi * 128, :],
                      resst[:, half:half + T1], waits=P.after("v"))

    q_p1_done = P.c("qy")
    cc_scan_in = {}

    arf.reset(MF_P1)
    tokm = arf.take(128)
    l2m = arf.take(128)
    is1m = arf.take(128)
    is2m = arf.take(128)
    m1r = arf.take(16)
    m2r = arf.take(16)
    qr = arf.take(16)
    w1r = arf.take(16)
    w2r = arf.take(16)
    combfm_a = arf.take(TPC)
    combfm = combfm_a[0:8, :]

    for c in range(16):
        P.mm(pb[7][:, 8 * c:8 * c + 8], logsb[:, 128 * c:128 * (c + 1)],
             idn[0:8, 0:8], True, True,
             waits=(P.after("s", "v") if c == 0 else ()),
             inc=("t" if c == 15 else None))
    P.vop("tensor_copy", tokm, pb[7][:, 0:128], waits=P.after("t"), inc="v")
    tv = tokm.rearrange("p (c e) -> p c e", e=8)
    P.vop("tensor_reduce", m1r, tv, AX.X, ALU.max, inc="v")
    m1b = m1r.rearrange("p (c o) -> p c o", o=1).broadcast_to([128, 16, 8])
    P.vop("tensor_tensor", is1m.rearrange("p (c e) -> p c e", e=8), tv, m1b,
          ALU.is_equal, inc="v")
    P.vop("scalar_tensor_tensor", l2m, is1m, -BIG, tokm, ALU.mult, ALU.add,
          inc="v")
    P.vop("tensor_reduce", m2r, l2m.rearrange("p (c e) -> p c e", e=8), AX.X,
          ALU.max, inc="v")
    m2b = m2r.rearrange("p (c o) -> p c o", o=1).broadcast_to([128, 16, 8])
    P.vop("tensor_tensor", is2m.rearrange("p (c e) -> p c e", e=8),
          l2m.rearrange("p (c e) -> p c e", e=8), m2b, ALU.is_equal, inc="v")
    P.vop("tensor_tensor", qr, m2r, m1r, ALU.subtract, inc="v")
    P.act(w1r, qr, AF.Sigmoid, scale=-1.0, waits=P.after("v"), inc="s")
    P.act(w2r, qr, AF.Sigmoid, scale=1.0, inc="s")
    P.vwait(P.after("s"))
    w1b = w1r.rearrange("p (c o) -> p c o", o=1).broadcast_to([128, 16, 8])
    w2b = w2r.rearrange("p (c o) -> p c o", o=1).broadcast_to([128, 16, 8])
    P.vop("tensor_tensor", is1m.rearrange("p (c e) -> p c e", e=8),
          is1m.rearrange("p (c e) -> p c e", e=8), w1b, ALU.mult, inc="v")
    P.vop("tensor_tensor", is2m.rearrange("p (c e) -> p c e", e=8),
          is2m.rearrange("p (c e) -> p c e", e=8), w2b, ALU.mult, inc="v")
    P.vop("tensor_tensor", is1m, is1m, is2m, ALU.add, inc="v")
    for cq in range(4):
        for j in range(4):
            c = 4 * cq + j
            P.mm(pb[5][0:8, 128 * j:128 * (j + 1)],
                 is1m[:, 8 * c:8 * c + 8], idn, True, True,
                 waits=(P.after("v", "s") if j == 0 else ()),
                 inc=("t" if j == 3 else None))
        P.vop("tensor_copy", combfm[:, 512 * cq:512 * (cq + 1)],
              pb[5][0:8, 0:512], waits=P.after("t"), inc="v")
    P.dma("gpsimd", comb_dr, combfm, waits=P.after("v"))

    wall_all()

    arf.reset(0)
    q1m = arf.take(TM)
    q1m2 = arf.take(TM)
    acc2 = [arf.take(8 * TM), arf.take(8 * TM)]
    combsb2 = [arf.take(TM), arf.take(TM)]
    arb.reset(0)
    xnb2 = [arb.take(8 * TM), arb.take(8 * TM)]
    wgb = arb.take(8 * FFN)
    wub = arb.take(8 * FFN)
    wdb = arb.take(12 * 1024)
    gsil = arb.take(TM)
    gsil2 = arb.take(TM)
    t2b = arb.take(12 * TM)

    s_cnt_gs = {}
    v_cnt_t2 = {}
    v_cnt_acc = {}
    t_gu_done = None
    t_dn_done = None
    q_str = None
    for half in range(2):
        for ti in range(2):
            tt = half * 2 + ti
            t0 = tt * TM
            for c in range(8):
                w0 = ()
                if ti == 0 and c == 0:
                    w0 = ([("t", t_gu_done)] if t_gu_done is not None
                          else P.after("t"))
                P.dma("sync", ch(xnb2[ti], c, TM), xnb_dr_v[c, :, t0:t0 + TM],
                      waits=w0)
        for e in range(NEXP + 1):
            off = e * EXP_W
            wg2 = moe_w(off, 1024, FFN).rearrange("(c p) m -> c p m", p=128)
            wu2 = moe_w(off + WG_SZ, 1024, FFN).rearrange(
                "(c p) m -> c p m", p=128)
            wd2 = moe_w(off + 2 * WG_SZ, FFN, 1024).rearrange(
                "(c p) m -> c p m", p=128)
            wts = ([("t", t_gu_done)] if t_gu_done is not None
                   else P.after("t")) + [("cc", CC_WMOE)]
            for k in range(8):
                P.dma("sync", ch(wgb, k, FFN), wg2[k],
                      waits=(wts if k == 0 else ()))
                P.dma("sync", ch(wub, k, FFN), wu2[k])
            for k in range(12):
                P.dma("sync", ch(wdb, k, 1024), wd2[k],
                      waits=([("t", t_dn_done)]
                             if (k == 0 and t_dn_done is not None) else ()))
            if e < NEXP:
                for ti in range(2):
                    tt = half * 2 + ti
                    P.dma("sync", combsb2[ti],
                          comb_dr[e:e + 1, tt * TM:(tt + 1) * TM]
                          .broadcast_to([128, TM]),
                          waits=(P.after("v") if ti == 0 else ()))
            ldt = P.c("qy")
            for ti in range(2):
                for mf in range(12):
                    par = mf % 2
                    bg, bu = (pb[0], pb[3]) if par == 0 else (pb[4], pb[5])
                    w_first = [("qy", ldt)]
                    if s_cnt_gs.get(par) is not None:
                        w_first.append(("s", s_cnt_gs[par]))
                        w_first.append(("v", v_cnt_t2[par]))
                    for k in range(8):
                        P.mm(bg[:, 0:TM],
                             wgb[:, k * FFN + mf * 128:k * FFN + (mf + 1) * 128],
                             ch(xnb2[ti], k, TM), start=(k == 0), stop=(k == 7),
                             waits=(w_first if k == 0 else ()))
                    for k in range(8):
                        P.mm(bu[:, 0:TM],
                             wub[:, k * FFN + mf * 128:k * FFN + (mf + 1) * 128],
                             ch(xnb2[ti], k, TM), start=(k == 0), stop=(k == 7),
                             inc=("t" if k == 7 else None))
                    gs = gsil if par == 0 else gsil2
                    q1 = q1m if par == 0 else q1m2
                    P.act(gs, bg[:, 0:TM], AF.Silu, waits=P.after("t"), inc="s")
                    s_cnt_gs[par] = P.c("s")
                    P.vop("tensor_tensor", q1, gs, bu[:, 0:TM], ALU.mult,
                          waits=P.after("s"), inc="v")
                    if e < NEXP:
                        P.vop("tensor_tensor", ch(t2b, mf, TM), q1,
                              combsb2[ti], ALU.mult, waits=[("qy", ldt)],
                              inc="v")
                    else:
                        P.vop("tensor_copy", ch(t2b, mf, TM), q1, inc="v")
                    v_cnt_t2[par] = P.c("v")
                P.vop("tensor_copy", drn[:, 0:1],
                      t2b[:, 12 * TM - 1:12 * TM], inc="v")
                v_all_t2 = P.c("v")
                if ti == 1:
                    t_gu_done = P.c("t")
                for mo in range(8):
                    par = mo % 2
                    bk = pb[1 + par]
                    w_first = [("v", v_all_t2)]
                    if v_cnt_acc.get(par) is not None:
                        w_first.append(("v", v_cnt_acc[par]))
                    for kc in range(12):
                        P.mm(bk[:, 0:TM],
                             wdb[:, kc * 1024 + mo * 128:kc * 1024 + (mo + 1) * 128],
                             ch(t2b, kc, TM), start=(kc == 0), stop=(kc == 11),
                             waits=(w_first if kc == 0 else ()),
                             inc=("t" if kc == 11 else None))
                    wv = [("t", P.c("t"))]
                    if e == 0 and mo == 0 and ti == 0 and q_str is not None:
                        wv.append(("qg", q_str))
                    if e == 0:
                        P.vop("tensor_copy", ch(acc2[ti], mo, TM),
                              bk[:, 0:TM], waits=wv, inc="v")
                    else:
                        P.vop("tensor_tensor", ch(acc2[ti], mo, TM),
                              ch(acc2[ti], mo, TM), bk[:, 0:TM], ALU.add,
                              waits=wv, inc="v")
                    v_cnt_acc[par] = P.c("v")
                if ti == 1:
                    t_dn_done = P.c("t")
            if half == 0:
                if e in (1, 3, 5, 7):
                    t = e // 2
                    collective("AllGather", uc_agi[t], uc_ago[t],
                               waits=([("qy", q_p1_done)] if t == 0 else ()))
                    cc_scan_in[("uc", t)] = P.c("cc")
                elif e == 8:
                    collective("AllGather", bc_agi[0], bc_ago[0])
                    collective("AllGather", dtr_agi[0], dtr_ago[0])
            elif e in (0, 1, 2):
                collective("AllGather", bc_agi[e + 1], bc_ago[e + 1])
                collective("AllGather", dtr_agi[e + 1], dtr_ago[e + 1])
        v_done = P.c("v")
        for ti in range(2):
            tt = half * 2 + ti
            for mo in range(8):
                P.dma("gpsimd", moe_dr_v[mo, :, tt * TM:(tt + 1) * TM],
                      ch(acc2[ti], mo, TM),
                      waits=([("v", v_done)] if (ti == 0 and mo == 0) else ()))
        q_str = P.c("qg")

    CC_SCANIN = P.c("cc")

    for t in range(4):
        def ucopy(e_, t=t):
            jD0, _ = _core_offs(e_)
            return e_.dma_start(out=uc_loc[t],
                                in_=uc_ago[t][:, ds(jD0, DSL), :])
        P.dyndma("sync", ucopy, waits=[("cc", cc_scan_in[("uc", t)])])
    wall_all()

    arf.reset(0)
    delta = arf.take(2 * TC)
    spa = arf.take(512)
    spb = arf.take(512)
    brep = arf.take(TC)
    crep = arf.take(TC)
    a_t2 = [arf.take(TC + 16), arf.take(TC + 16)]
    b_t = arf.take(TC + 16)
    h_t2 = [arf.take(TC + 16), arf.take(TC + 16)]
    spc = arf.take(16)
    yst = arf.take(512)
    arb.reset(0)
    ucs = arb.take(2 * TC)
    delb = arb.take(2 * TC)
    dub = arb.take(2 * TC)
    ypb2 = [arb.take(TC), arb.take(TC)]
    dtrsl_a = arb.take(TC)
    dtrsl = dtrsl_a[0:DTR, :]

    wstg4 = arf.take(2048)
    wopb = arb.take(16 * 1024)
    wf1b = arb.take(8 * 1024)
    wf2b = arb.take(8 * 1024)
    w_op_v = w_op.rearrange("(c p) m -> c p m", p=128)
    w_f1_v = w_f1.rearrange("(c p) m -> c p m", p=128)
    w_f2_v = w_f2.rearrange("(c p) m -> c p m", p=128)
    w4q = ([(wopb, w_op_v, c) for c in range(16)]
           + [b for c in range(8)
              for b in ((wf1b, w_f1_v, c), (wf2b, w_f2_v, c))])
    w4_state = {"n": 0}
    w4_scopy = []
    ws_free = [None, None]

    def w4_dma(k):
        for _ in range(k):
            n = w4_state["n"]
            if n >= len(w4q):
                return
            dstb, srcv, c = w4q[n]
            par = n % 2
            stg = wstg4[:, par * 1024:(par + 1) * 1024]
            w = [("s", ws_free[par])] if ws_free[par] is not None else []
            P.dma("sync", stg, srcv[c], waits=w)
            w4_scopy.append((ch(dstb, c, 1024), stg, P.c("qy"), par))
            w4_state["n"] = n + 1

    def w4_flush():
        for dst, stg, q, par in w4_scopy:
            P.scopy(dst, stg, waits=[("qy", q)], inc="s")
            ws_free[par] = P.c("s")
        w4_scopy.clear()

    P.vop("memset", a_t2[0][:, TC:TC + 1], 0.0, waits=P.after("s", "t"), inc="v")
    P.vop("memset", a_t2[1][:, TC:TC + 1], 0.0, inc="v")
    P.vop("memset", b_t[:, TC:TC + 1], 0.0, inc="v")
    pend_g = None

    db_banks = [pb[5], pb[3], pb[4]]
    du_banks = [pb[6], pb[2], pb[1], pb[0]]
    s_exp = {i: None for i in range(3)}
    v_bt = {i: None for i in range(4)}
    v_scan = {0: None, 1: None}
    s_yev = None
    q_ydma = None
    subidx = 0
    pend_ysel = None

    def emit_ysel(g, tc, v_ym):
        nonlocal s_yev, q_ydma
        for su in range(4):
            w = [("v", P.c("v"))] if su == 0 else []
            if su == 0 and s_yev is not None:
                w.append(("s", s_yev))
            P.mm(pb[7][32 * su:32 * su + 8, 0:512], ysl,
                 ypb2[g % 2][:, su * 512:(su + 1) * 512],
                 True, True, tp=(0, 32 * su), waits=w,
                 inc=("t" if su == 3 else None))
        wq = P.after("t")
        if q_ydma is not None:
            wq.append(("qy", q_ydma))
        P.scopy(yst[:, 0:512], pb[7][:, 0:512], waits=wq, inc="s")
        s_yev = P.c("s")
        for su in range(4):
            P.dma("sync",
                  y_agi[tc, 8 * g:8 * g + 8, su * 512:(su + 1) * 512],
                  yst[32 * su:32 * su + 8, 0:512],
                  waits=(P.after("s") if su == 0 else ()))
        q_ydma = P.c("qy")

    for tc in range(NTC):
        for t in range(4):
            w0 = ()
            if t == 0:
                w0 = P.after("s", "t")
                if tc == 0:
                    w0 = w0 + [("cc", CC_SCANIN)]
            P.dma("sync", dtrsl[:, t * T1:(t + 1) * T1], dtr_ago[t, tc],
                  waits=w0)
        for dd in range(2):
            for t in range(4):
                P.dma("sync", ch(ucs, dd, TC)[:, t * T1:(t + 1) * T1],
                      uc_loc[t, tc, dd * 128:(dd + 1) * 128, :],
                      waits=(P.after("v") if (dd == 0 and t == 0) else ()))
        for t in range(4):
            P.dma("sync", brep[:, t * T1:(t + 1) * T1], bc_ago[t, tc, 0:128, :],
                  waits=(P.after("v") if t == 0 else ()))
            P.dma("sync", crep[:, t * T1:(t + 1) * T1],
                  bc_ago[t, tc, 128:256, :])
        w4_flush()
        ld0 = P.c("qy")
        w4_dma(2)

        for dd in range(2):
            for su in range(4):
                par = subidx % 3
                subidx += 1
                bk = db_banks[par]
                w = [("qy", ld0)]
                if s_exp[par] is not None:
                    w.append(("s", s_exp[par]))
                w += P.after("v")
                P.mm(bk[:, 0:512], wdtb[:, dd * 128:(dd + 1) * 128],
                     dtrsl[:, su * 512:(su + 1) * 512], True, True,
                     waits=w, inc="t")
                dcol = delta[:, dd * TC + su * 512: dd * TC + (su + 1) * 512]
                P.act(spa, bk[:, 0:512], AF.Abs, bias=dtb[:, dd:dd + 1],
                      waits=P.after("t"), inc="s")
                P.act(spa, spa, AF.Exp, scale=-1.0, inc="s")
                P.act(spa, spa, AF.Ln, bias=1.0, inc="s")
                s_exp[par] = P.c("s")
                P.vop("tensor_scalar", spb, bk[:, 0:512], dtb[:, dd:dd + 1],
                      0.0, ALU.add, ALU.max, waits=P.after("s"), inc="v")
                P.vop("tensor_tensor", dcol, spa, spb, ALU.add, inc="v")
            dD = ch(delta, dd, TC)
            P.vop("tensor_tensor", ch(dub, dd, TC), dD, ch(ucs, dd, TC),
                  ALU.mult, waits=P.after("qy"), inc="v")
            P.vop("tensor_copy", ch(delb, dd, TC), dD, inc="v")
        P.vop("tensor_copy", drn[:, 0:1], dub[:, 2 * TC - 1:2 * TC], inc="v")
        P.vop("tensor_copy", drn[:, 1:2], delb[:, 2 * TC - 1:2 * TC], inc="v")
        v_dt = P.c("v")

        if tc == 0 or tc == NTC // 2:
            P.vop("memset", hst, 0.0, inc="v")
            pend_g = None

        for g in range(32):
            if g == 16:
                w4_flush()
                w4_dma(2)
            dd, gg = g // 16, g % 16
            Gb, cc_ = gg // 4, gg % 4
            tp = (32 * Gb, 0)
            selp = sel[32 * Gb:32 * Gb + 32, cc_ * 128:(cc_ + 1) * 128]
            at = a_t2[g % 2]
            for su in range(4):
                pd = subidx % 3
                pu = subidx % 4
                subidx += 1
                rng = slice(su * 512, (su + 1) * 512)
                wdb_ = [] if s_exp[pd] is None else [("s", s_exp[pd])]
                if su == 0:
                    wdb_.append(("v", v_dt))
                    if v_scan[g % 2] is not None:
                        wdb_.append(("v", v_scan[g % 2]))
                P.mm(db_banks[pd][:, 0:512], selp,
                     ch(delb, dd, TC)[32 * Gb:32 * Gb + 32, rng], True, True,
                     tp=tp, waits=wdb_, inc="t")
                wdu_ = [] if v_bt[pu] is None else [("v", v_bt[pu])]
                P.mm(du_banks[pu][:, 0:512], selp,
                     ch(dub, dd, TC)[32 * Gb:32 * Gb + 32, rng], True, True,
                     tp=tp, waits=wdu_, inc="t")
                P.act(at[:, rng], db_banks[pd][:, 0:512], AF.Exp,
                      scale=aneg[:, g:g + 1], waits=P.after("t"), inc="s")
                s_exp[pd] = P.c("s")
                P.vop("tensor_tensor", b_t[:, rng], du_banks[pu][:, 0:512],
                      brep[:, rng], ALU.mult, waits=P.after("t"), inc="v")
                v_bt[pu] = P.c("v")
            if pend_ysel is not None:
                flush_tc = pend_ysel[1]
                emit_ysel(*pend_ysel)
                pend_ysel = None
                if g == 0 and tc > 0:
                    collective("AllGather", y_agi[tc - 1],
                               y_ago[tc - 1], waits=P.after("qy"))
            P.vop("tensor_tensor_scan", h_t2[g % 2][:, 0:TC + 1],
                  at[:, 0:TC + 1], b_t[:, 0:TC + 1], hst[:, g:g + 1],
                  ALU.mult, ALU.add, waits=[("s", P.c("s"))], inc="v")
            v_scan[g % 2] = P.c("v")
            P.vop("memset", spc[:, 0:1], 0.0, inc="v")
            if pend_g is not None:
                P.vop("tensor_copy", hst[:, pend_g:pend_g + 1],
                      h_t2[pend_g % 2][:, TC - 1:TC], inc="v")
            pend_g = g
            P.vop("tensor_tensor", ypb2[g % 2], h_t2[g % 2][:, 0:TC], crep,
                  ALU.mult, inc="v")
            P.vop("tensor_copy", spc[:, 1:2], ypb2[g % 2][:, TC - 1:TC],
                  inc="v")
            v_ym = P.c("v")
            pend_ysel = (g, tc, v_ym)

    if pend_ysel is not None:
        emit_ysel(*pend_ysel)
        pend_ysel = None
    w4_flush()
    collective("AllGather", y_agi[NTC - 1],
               y_ago[NTC - 1], waits=P.after("qy"))
    CC_Y = P.c("cc")
    wall_all()

    arf.reset(0)
    y4d = [arf.take(16 * T4), arf.take(16 * T4)]
    res4d = [arf.take(16 * T4), arf.take(16 * T4)]
    o4d = [arf.take(8 * T4), arf.take(8 * T4)]
    xn4 = arf.take(8 * T4)
    sq4 = arf.take(T4)
    st4_a = arf.take(4 * T4)
    st4 = st4_a[0:1, :]
    arb.reset(0)
    uc4 = arb.take(16 * T4)
    ytbd = [arb.take(16 * T4), arb.take(16 * T4)]
    ob4 = arb.take(8 * T4)
    h1b = arb.take(8 * T4)

    def srow4(slot, n=T4):
        return st4[:, slot * T4:slot * T4 + n]

    NT4 = TPC // T4
    ld1 = {}
    ld2 = {}
    ld3 = {}
    qg_out = {}
    v_zdone = {}
    v_fin = {}

    def emit_L1(it):
        p = it % 2
        t0 = it * T4

        def ycopy(e, t0=t0):
            j = _core_j(e)
            return e.dma_start(out=y_loc[:, :, t0:t0 + T4],
                               in_=y_ago[ds(j, 1), :, :, t0:t0 + T4])
        P.dyndma("sync", ycopy, waits=([("cc", CC_Y)] if it == 0 else ()))
        cpy = P.c("qy")
        for c in range(16):
            w = [("qy", cpy)] if c == 0 else ()
            if c == 0 and it >= 2:
                w = w + [("qg", qg_out[it - 2])]
            P.dma("sync", ch(y4d[p], c, T4),
                  y_loc[c // 2, 128 * (c % 2):128 * (c % 2) + 128,
                        t0:t0 + T4], waits=w)
        for c in range(16):
            P.dma("sync", ch(res4d[p], c, T4), res_dr_v[c, :, t0:t0 + T4])
            P.dma("sync", ch(uc4, c, T4),
                  uc_agi_v[it // 2, c, :, (it % 2) * T4:(it % 2) * T4 + T4])
        ld1[it] = P.c("qy")

    def emit_L2(it, wv):
        t0 = it * T4
        for c in range(8):
            P.dma("sync", ch(xn4, c, T4), xn_dr_v[c, :, t0:t0 + T4],
                  waits=([("v", wv)] if (c == 0 and wv is not None) else ()))
        ld2[it] = P.c("qy")

    emit_L1(0)
    emit_L2(0, None)

    for it in range(NT4):
        p = it % 2
        t0 = it * T4
        bb = 0 if p == 0 else 4
        z4 = y4d[p]
        x4v = lambda c: res4d[p][:, c * T4:(c + 1) * T4]
        mo4v = lambda c: res4d[p][:, 8 * T4 + c * T4:8 * T4 + (c + 1) * T4]

        P.act(res4d[p], res4d[p], AF.Silu, waits=[("qy", ld1[it])], inc="s")
        s_sil = P.c("s")
        for c in range(16):
            P.vop("scalar_tensor_tensor", ch(y4d[p], c, T4), ch(uc4, c, T4),
                  dpc[:, c:c + 1], ch(y4d[p], c, T4), ALU.mult, ALU.add,
                  waits=([("qy", ld1[it])] if c == 0 else ()), inc="v")
        P.vop("tensor_tensor", ytbd[p], y4d[p], res4d[p], ALU.mult,
              waits=[("s", s_sil)], inc="v")
        P.vop("tensor_copy", drn[:, 0:1],
              ytbd[p][:, 16 * T4 - 1:16 * T4], inc="v")
        v_ytb = P.c("v")

        for c in range(8):
            P.dma("sync", x4v(c), x_fm_v[c, :, 3 + t0:3 + t0 + T4],
                  waits=([("v", v_ytb)] if c == 0 else ()))
        for c in range(8):
            P.dma("sync", mo4v(c), moe_dr_v[c, :, t0:t0 + T4])
        ld3[it] = P.c("qy")
        if it + 1 < NT4:
            emit_L1(it + 1)

        ope_v = {}
        for m in range(8):
            bk = pb[bb + (m % 2)]
            if m == 0:
                w0 = [("v", max(v_ytb, v_zdone.get(it - 2, 0)))]
            elif m >= 2:
                w0 = [("v", ope_v[m - 2])]
            else:
                w0 = ()
            for k in range(16):
                P.mm(bk[:, 0:T4],
                     wopb[:, k * 1024 + m * 128:k * 1024 + (m + 1) * 128],
                     ch(ytbd[p], k, T4), start=(k == 0), stop=(k == 15),
                     waits=(w0 if k == 0 else ()),
                     inc=("t" if k == 15 else None))
            t_op = P.c("t")
            P.vop("tensor_tensor", ch(o4d[p], m, T4), bk[:, 0:T4],
                  ch(xn4, m, T4), ALU.add,
                  waits=[("t", t_op)] + ([("qy", ld2[it])] if m == 0 else []),
                  inc="v")
            P.vop("tensor_copy", ch(ob4, m, T4), ch(o4d[p], m, T4), inc="v")
            ope_v[m] = P.c("v")
        P.vop("tensor_copy", drn[:, 0:1], ob4[:, 8 * T4 - 1:8 * T4], inc="v")
        v_ob_all = P.c("v")
        if it + 1 < NT4:
            emit_L2(it + 1, ope_v[7])

        s_gelu = {}
        for m in range(8):
            bk = pb[bb + (m % 2)]
            if m == 0:
                w0 = [("v", v_ob_all)]
            elif m >= 2:
                w0 = [("s", s_gelu[m - 2])]
            else:
                w0 = ()
            for k in range(8):
                P.mm(bk[:, 0:T4],
                     wf1b[:, k * 1024 + m * 128:k * 1024 + (m + 1) * 128],
                     ch(ob4, k, T4), start=(k == 0), stop=(k == 7),
                     waits=(w0 if k == 0 else ()),
                     inc=("t" if k == 7 else None))
            P.act(ch(h1b, m, T4), bk[:, 0:T4], AF.Gelu, bias=b1c[:, m:m + 1],
                  waits=[("t", P.c("t"))], inc="s")
            s_gelu[m] = P.c("s")
        s_gelu_all = P.c("s")

        zadd_v = {}
        for m in range(8):
            bk = pb[bb + (m % 2)]
            if m == 0:
                w0 = [("s", s_gelu_all)]
            elif m >= 2:
                w0 = [("v", zadd_v[m - 2])]
            else:
                w0 = ()
            for k in range(8):
                P.mm(bk[:, 0:T4],
                     wf2b[:, k * 1024 + m * 128:k * 1024 + (m + 1) * 128],
                     ch(h1b, k, T4), start=(k == 0), stop=(k == 7),
                     waits=(w0 if k == 0 else ()),
                     inc=("t" if k == 7 else None))
            P.vop("tensor_scalar_add", ch(z4, m, T4), bk[:, 0:T4],
                  b2c[:, m:m + 1], waits=[("t", P.c("t"))], inc="v")
            P.vop("tensor_tensor", ch(z4, m, T4), ch(z4, m, T4),
                  ch(o4d[p], m, T4), ALU.add, inc="v")
            zadd_v[m] = P.c("v")
        P.vop("tensor_copy", drn[:, 0:1], z4[:, 8 * T4 - 1:8 * T4], inc="v")
        v_zdone[it] = P.c("v")

        for c in range(8):
            P.mm(pb[2][0:1, 0:T4], onk, ch(z4, c, T4), start=(c == 0),
                 stop=(c == 7),
                 waits=([("v", v_zdone[it])] if c == 0 else ()),
                 inc=("t" if c == 7 else None))
        for c in range(8):
            P.act(sq4, ch(z4, c, T4), AF.Square, waits=P.after("t"), inc="s")
            P.mm(pb[3][0:1, 0:T4], onk, sq4, start=(c == 0), stop=(c == 7),
                 waits=P.after("s"), inc="t")
        mu4, v4 = srow4(0), srow4(1)
        r4, mr4 = srow4(2), srow4(3)
        P.vop("tensor_scalar_mul", mu4, pb[2][0:1, 0:T4], 1.0 / D,
              waits=P.after("t"), inc="v")
        P.vop("tensor_tensor", v4, mu4, mu4, ALU.mult, inc="v")
        P.vop("scalar_tensor_tensor", v4, pb[3][0:1, 0:T4], 1.0 / D, v4,
              ALU.mult, ALU.subtract, inc="v")
        P.act(r4, v4, AF.Ln, bias=EPS, waits=P.after("v"), inc="s")
        P.act(r4, r4, AF.Exp, scale=-0.5, inc="s")
        P.vop("tensor_tensor", mr4, mu4, r4, ALU.mult, waits=P.after("s"),
              inc="v")
        P.mm(pb[6][:, 0:T4], on1, r4, True, True, waits=P.after("v"), inc="t")
        P.mm(pb[7][:, 0:T4], on1, mr4, True, True, inc="t")
        t_bc = P.c("t")
        for c in range(8):
            w0 = []
            if c == 0:
                w0 = [("t", t_bc), ("qy", ld3[it])]
            P.vop("tensor_tensor", ch(z4, c, T4), ch(z4, c, T4),
                  pb[6][:, 0:T4], ALU.mult, waits=w0, inc="v")
            P.vop("tensor_tensor", ch(z4, c, T4), ch(z4, c, T4),
                  pb[7][:, 0:T4], ALU.subtract, inc="v")
            P.vop("tensor_scalar", ch(z4, c, T4), ch(z4, c, T4),
                  lnw[:, c:c + 1], lnb[:, c:c + 1], ALU.mult, ALU.add,
                  inc="v")
            P.vop("tensor_tensor", ch(z4, c, T4), ch(z4, c, T4), x4v(c),
                  ALU.add, inc="v")
            P.vop("tensor_tensor", ch(z4, c, T4), ch(z4, c, T4), mo4v(c),
                  ALU.add, inc="v")
            P.dma("gpsimd", out_fm_v[c, :, t0:t0 + T4], ch(z4, c, T4),
                  waits=P.after("v"))
        qg_out[it] = P.c("qg")
        v_fin[it] = P.c("v")

    for t, src in taps:
        P.dma("gpsimd", t, src, waits=P.after("qg", "v", "s", "t"))

    P.step("gpsimd", None, waits=P.after("qg", "qy"))
    P.step("sync", None, waits=P.after("qg"))

    with es, P.es:
        with nc.Block() as block:
            P.emit(block)

    return nc



def _prep_inputs(a):
    import ml_dtypes
    bf16 = ml_dtypes.bfloat16
    f32 = np.float32

    x = np.asarray(a["L0_attn_x"], f32)
    norm_w = np.asarray(a["L0_attn_norm_w"], f32)
    in_proj_w = np.asarray(a["L0_attn_in_proj_w"], f32)
    conv_w = np.asarray(a["L0_attn_conv_w"], f32)
    conv_b = np.asarray(a["L0_attn_conv_b"], f32)
    x_proj_w = np.asarray(a["L0_attn_x_proj_w"], f32)
    dt_w = np.asarray(a["L0_attn_dt_w"], f32)
    dt_b = np.asarray(a["L0_attn_dt_b"], f32)
    A_log = np.asarray(a["L0_attn_A_log"], f32)
    Dp = np.asarray(a["L0_attn_Dp"], f32)
    out_proj_w = np.asarray(a["L0_attn_out_proj_w"], f32)
    ff_w1 = np.asarray(a["L0_attn_ff_w1"], f32)
    ff_b1 = np.asarray(a["L0_attn_ff_b1"], f32)
    ff_w2 = np.asarray(a["L0_attn_ff_w2"], f32)
    ff_b2 = np.asarray(a["L0_attn_ff_b2"], f32)
    ff_ln_w = np.asarray(a["L0_attn_ff_ln_w"], f32)
    ff_ln_b = np.asarray(a["L0_attn_ff_ln_b"], f32)
    gate_w = np.asarray(a["L0_ffn_gate_w"], f32)
    w_up = np.asarray(a["L0_ffn_w_up"], f32)
    w_down = np.asarray(a["L0_ffn_w_down"], f32)
    sg_gate = np.asarray(a["L0_ffn_sg_gate"], f32)
    sg_up = np.asarray(a["L0_ffn_sg_up"], f32)
    sg_down = np.asarray(a["L0_ffn_sg_down"], f32)

    C = np.ascontiguousarray

    def col(v, nchunk):
        return C(v.reshape(nchunk, 128).T.astype(f32))

    shared = {
        "w_ip": C(in_proj_w.T),
        "nrm_col": col(norm_w, 8),
        "cw_col": C(conv_w.reshape(16, 128, 4).transpose(1, 0, 2).reshape(128, 64)),
        "cb_col": col(conv_b, 16),
        "dp_col": col(Dp, 16),
        "w_op": C(out_proj_w.T),
        "w_f1": C(ff_w1.T), "b1_col": col(ff_b1, 8),
        "w_f2": C(ff_w2.T), "b2_col": col(ff_b2, 8),
        "lnw_col": col(ff_ln_w, 8), "lnb_col": col(ff_ln_b, 8),
        "w_gate": C(gate_w.T.reshape(8, 128, 8).transpose(1, 0, 2).reshape(128, 64)),
        "ones1": np.ones((1, 128), f32),
        "onesk": np.ones((128, 1), f32),
        "ident": np.eye(128, dtype=f32),
    }
    xp_wide = np.concatenate([x_proj_w[0:64],
                              np.tile(x_proj_w[64:80], (8, 1)),
                              np.tile(x_proj_w[80:96], (8, 1))], axis=0)
    shared["w_xp"] = C(xp_wide.T)
    base = np.zeros((32, 512), f32)
    for c in range(4):
        for p in range(128):
            base[8 * c + p // 16, c * 128 + p] = 1.0
    shared["sel4"] = C(np.tile(base, (4, 1)).astype(bf16))
    ys = np.zeros((128, 8), f32)
    for p in range(128):
        ys[p, p // 16] = 1.0
    shared["ysel_c"] = C(ys.astype(bf16))

    parts = []
    for e in range(NEXP):
        parts.append(C(w_up[e, :FFN, :].T).ravel())
        parts.append(C(w_up[e, FFN:, :].T).ravel())
        parts.append(C(w_down[e].T).ravel())
    parts.append(C(sg_gate.T).ravel())
    parts.append(C(sg_up.T).ravel())
    parts.append(C(sg_down.T).ravel())
    moe_flat = np.concatenate(parts).astype(f32)
    assert moe_flat.size == MOE_TOT

    in_maps = []
    for j in range(NC):
        b, p0 = j // 4, (j % 4) * TPC
        xs = np.zeros((D, TPC + 3), f32)
        if p0 == 0:
            xs[:, 3:] = x[b, 0:TPC].T
        else:
            xs[:, :] = x[b, p0 - 3:p0 + TPC].T
        sl = slice(DSL * j, DSL * (j + 1))
        al = A_log[sl].reshape(32, 8, 16).transpose(1, 2, 0).reshape(128, 32)
        m = dict(shared)
        m["x_fm"] = xs
        m["w_dt"] = C(dt_w[sl].T)
        m["dtb_col"] = col(dt_b[sl], 2)
        m["alog_pk"] = C(al)
        m["w_moe_sl"] = C(moe_flat[j * MOE_SL:(j + 1) * MOE_SL]
                          .reshape(MOE_SL // 2048, 2048))
        in_maps.append(m)
    return in_maps


_NC_CACHE = {}


def kernel(**inputs):
    in_maps = _prep_inputs(inputs)
    if "prog" not in _NC_CACHE:
        _NC_CACHE["prog"] = build_program(debug=False)
    nc = _NC_CACHE["prog"]
    res = run_bass_kernel_spmd(nc, in_maps, core_ids=list(range(NC)))
    out = np.empty((B, L, D), np.float32)
    for j in range(NC):
        b, p0 = j // 4, (j % 4) * TPC
        out[b, p0:p0 + TPC, :] = res.results[j]["out_fm"].T
    return out


# version 17
# speedup vs baseline: 1.0616x; 1.0088x over previous
import contextlib
import numpy as np

import concourse.bass as bass
import concourse.mybir as mybir
from concourse.bass_utils import run_bass_kernel_spmd

F32 = mybir.dt.float32
BF16 = mybir.dt.bfloat16
AF = mybir.ActivationFunctionType
ALU = mybir.AluOpType
AX = mybir.AxisListType
ds = bass.ds

NC = 8
B, L, D, DI, NST, DTR = 2, 8192, 1024, 2048, 16, 64
TOK = B * L
TPC = TOK // NC
DSL = DI // NC
T1 = 512
W1 = T1 + 3
TM = 512
T4 = 256
NEXP = 8
FFN = 1536
EPS = 1e-5
BIG = 1.0e30
TC = 2048
NTC = TOK // TC

EXP_W = 3 * 1024 * FFN
MOE_TOT = (NEXP + 1) * EXP_W
MOE_SL = MOE_TOT // NC
WG_SZ = 1024 * FFN

AR_F = 24000
AR_B = 54000

ENGS = ("sync", "scalar", "vector", "tensor", "gpsimd")


class Prog:
    def __init__(self, nc):
        self.nc = nc
        self.es = contextlib.ExitStack()
        self.sems = {}
        self.cnt = {}
        self.steps = {e: [] for e in ENGS}

    def sem(self, name):
        if name not in self.sems:
            self.sems[name] = self.es.enter_context(self.nc.semaphore(name))
            self.cnt[name] = 0
        return self.sems[name]

    def step(self, eng, fn=None, waits=(), inc=None, inc_by=1):
        if inc is not None:
            self.sem(inc)
            self.cnt[inc] += inc_by
        for s, _ in waits:
            self.sem(s)
        self.steps[eng].append((fn, tuple(waits), inc, inc_by))

    def c(self, name):
        self.sem(name)
        return self.cnt[name]

    def after(self, *names):
        return [(n, self.c(n)) for n in names]

    def dma(self, eng, out, in_, waits=(), accum=None):
        q = "qy" if eng == "sync" else "qg"

        def f(e, out=out, in_=in_, accum=accum):
            if accum is None:
                return e.dma_start(out=out, in_=in_)
            return e.dma_start(out=out, in_=in_, accum_op=accum)
        self.step(eng, f, waits=waits, inc=q, inc_by=16)

    def dyndma(self, eng, fn, waits=()):
        q = "qy" if eng == "sync" else "qg"
        self.step(eng, fn, waits=waits, inc=q, inc_by=16)

    def mm(self, out, lhsT, rhs, start, stop, tp=None, waits=(), inc=None):
        def f(e, out=out, lhsT=lhsT, rhs=rhs, start=start, stop=stop, tp=tp):
            if tp is None:
                return e.matmul(out, lhsT, rhs, start=start, stop=stop)
            return e.matmul(out, lhsT, rhs, start=start, stop=stop,
                            tile_position=tp)
        self.step("tensor", f, waits=waits, inc=inc)

    def trans(self, out, in_, ident, waits=(), inc=None):
        def f(e, out=out, in_=in_, ident=ident):
            return e.transpose(out, in_, ident)
        self.step("tensor", f, waits=waits, inc=inc)

    def act(self, out, in_, func, bias=0.0, scale=1.0, waits=(), inc=None):
        def f(e, out=out, in_=in_, func=func, bias=bias, scale=scale):
            return e.activation(out, in_, func, bias=bias, scale=scale)
        self.step("scalar", f, waits=waits, inc=inc)

    def scopy(self, out, in_, waits=(), inc=None):
        def f(e, out=out, in_=in_):
            return e.copy(out, in_)
        self.step("scalar", f, waits=waits, inc=inc)

    def vop(self, name, *args, waits=(), inc=None):
        def f(e, name=name, args=args):
            return getattr(e, name)(*args)
        self.step("vector", f, waits=waits, inc=inc)

    def vwait(self, waits):
        self.step("vector", None, waits=waits)

    def emit(self, block):
        def make(eng_name):
            def body(eng):
                for fn, waits, inc, inc_by in self.steps[eng_name]:
                    for s, v in waits:
                        eng.wait_ge(self.sems[s], v)
                    ins = fn(eng) if fn is not None else None
                    if inc is not None:
                        if ins is None:
                            ins = eng.nop()
                        ins.then_inc(self.sems[inc], inc_by)
            return body
        block.sync(make("sync"))
        block.scalar(make("scalar"))
        block.vector(make("vector"))
        block.tensor(make("tensor"))
        block.gpsimd(make("gpsimd"))


class Bump:
    def __init__(self, ap, elems, name):
        self.ap, self.n, self.off, self.name = ap, elems, 0, name

    def take(self, elems):
        elems = int(elems)
        off = (self.off + 15) // 16 * 16
        v = self.ap[:, off:off + elems]
        self.off = off + elems
        assert self.off <= self.n, f"{self.name} overflow {self.off}>{self.n}"
        return v

    def mark(self):
        return self.off

    def reset(self, m=0):
        self.off = m


def ch(ap_, c, w):
    return ap_[:, c * w:(c + 1) * w]


def _core_offs(e):
    if not hasattr(e, "_mk_offs"):
        j = e.partition_id()
        jD0 = e.snap(j * DSL, min_val=0, max_val=(NC - 1) * DSL)
        jT0 = e.snap(j * TPC, min_val=0, max_val=(NC - 1) * TPC)
        e._mk_offs = (jD0, jT0)
    return e._mk_offs


def _core_j(e):
    if not hasattr(e, "_mk_j"):
        e._mk_j = e.snap(e.partition_id(), min_val=0, max_val=NC - 1)
    return e._mk_j


def build_program(debug=False):
    nc = bass.Bass("TRN2", target_bir_lowering=False)

    def din(name, shape, dt=F32):
        return nc.dram_tensor(name, shape, dt, kind="ExternalInput").ap()

    def dint(name, shape, dt=F32):
        return nc.dram_tensor(name, shape, dt).ap()

    x_fm = din("x_fm", [D, TPC + 3])
    w_ip = din("w_ip", [D, 2 * DI])
    nrm_col = din("nrm_col", [128, D // 128])
    cw_col = din("cw_col", [128, (DI // 128) * 4])
    cb_col = din("cb_col", [128, DI // 128])
    w_xp = din("w_xp", [DI, 320])
    w_dt = din("w_dt", [DTR, DSL])
    dtb_col = din("dtb_col", [128, DSL // 128])
    alog_pk = din("alog_pk", [128, DSL // 8])
    dp_col = din("dp_col", [128, DI // 128])
    w_op = din("w_op", [DI, D])
    w_f1 = din("w_f1", [D, D])
    b1_col = din("b1_col", [128, D // 128])
    w_f2 = din("w_f2", [D, D])
    b2_col = din("b2_col", [128, D // 128])
    lnw_col = din("lnw_col", [128, D // 128])
    lnb_col = din("lnb_col", [128, D // 128])
    w_gate = din("w_gate", [128, (D // 128) * 8])
    w_moe_sl = din("w_moe_sl", [MOE_SL // 2048, 2048], F32)
    sel4 = din("sel4", [128, 512], BF16)
    ysel_c = din("ysel_c", [128, 8], BF16)
    ones1 = din("ones1", [1, 128], F32)
    onesk = din("onesk", [128, 1], F32)
    ident = din("ident", [128, 128], F32)

    out_fm = nc.dram_tensor("out_fm", [D, TPC], F32, kind="ExternalOutput").ap()

    uc_agi = dint("uc_agi", [4, DI, T1], BF16)
    uc_ago = dint("uc_ago", [4, NC, DI, T1], BF16)
    bc_agi = dint("bc_agi", [4, 256, T1], F32)
    bc_ago = dint("bc_ago", [4, NC, 256, T1], F32)
    dtr_agi = dint("dtr_agi", [4, DTR, T1], BF16)
    dtr_ago = dint("dtr_ago", [4, NC, DTR, T1], BF16)
    wm_agi = dint("wm_agi", [MOE_SL // 2048, 2048], BF16)
    wm_ago = dint("wm_ago", [NC, MOE_SL], BF16)
    y_agi = dint("y_agi", [NTC, DSL, TC], F32)
    y_ago = dint("y_ago", [NTC, NC, DSL, TC], F32)
    uc_loc = dint("uc_loc", [4, NC, DSL, T1], BF16)
    res_dr = dint("res_dr", [DI, TPC], F32)
    xn_dr = dint("xn_dr", [D, TPC], F32)
    xnb_dr = dint("xnb_dr", [D, TPC], BF16)
    comb_dr = dint("comb_dr", [NEXP, TPC], F32)
    moe_dr = dint("moe_dr", [D, TPC], F32)
    y_loc = dint("y_loc", [NC, DSL, TPC], F32)

    wm_flat = wm_ago.rearrange("a b -> (a b)")

    def moe_w(off, k, m):
        return wm_flat[off:off + k * m].rearrange("(k m) -> k m", m=m)

    es = contextlib.ExitStack()

    def sb(name, shape, dt):
        return es.enter_context(nc.sbuf_tensor(name, shape, dt)).ap()

    nrm = sb("nrm", [128, D // 128], F32)
    cw = sb("cw", [128, (DI // 128) * 4], F32)
    cbx = sb("cbx", [128, DI // 128], F32)
    dtb = sb("dtb", [128, DSL // 128], F32)
    aneg = sb("aneg", [128, DSL // 8], F32)
    dpc = sb("dpc", [128, DI // 128], F32)
    b1c = sb("b1c", [128, D // 128], F32)
    b2c = sb("b2c", [128, D // 128], F32)
    lnw = sb("lnw", [128, D // 128], F32)
    lnb = sb("lnb", [128, D // 128], F32)
    sel = sb("sel", [128, 512], BF16)
    ysl = sb("ysl", [128, 8], BF16)
    on1 = sb("on1", [1, 128], F32)
    onk = sb("onk", [128, 1], F32)
    idn = sb("idn", [128, 128], F32)
    wgt8 = sb("wgt8", [128, (D // 128) * 8], F32)
    wdtb = sb("wdtb", [DTR, DSL], BF16)
    hst = sb("hst", [128, DSL // 8], F32)
    drn = sb("drn", [128, 2], F32)

    arf_t = sb("arf", [128, AR_F], F32)
    arb_t = sb("arb", [128, AR_B], BF16)
    arf = Bump(arf_t, AR_F, "arf")
    arb = Bump(arb_t, AR_B, "arb")

    cst_eps = sb("cst_eps", [128, 1], F32)
    nc.const_aps.register(F32, EPS, cst_eps)

    pb = [es.enter_context(nc.psum_tensor(f"pb{i}", [128, 512], F32)).ap()
          for i in range(8)]

    P = Prog(nc)

    def wall_all():
        w = P.after("qy", "qg", "s", "v", "t")
        for engn in ENGS:
            P.step(engn, None, waits=w)

    taps = []
    if debug:
        for nm, src, shape, dt in (
                ("xn", xn_dr, [D, TPC], F32),
                ("uc", uc_agi, [4, DI, T1], BF16),
                ("bc", bc_agi, [4, 256, T1], F32),
                ("dtr", dtr_agi, [4, DTR, T1], BF16),
                ("res", res_dr, [DI, TPC], F32),
                ("comb", comb_dr, [NEXP, TPC], F32),
                ("y", y_agi, [NTC, DSL, TC], F32),
                ("moe", moe_dr, [D, TPC], F32),
                ("xnb", xnb_dr, [D, TPC], BF16)):
            t = nc.dram_tensor("tap_" + nm, shape, dt, kind="ExternalOutput").ap()
            taps.append((t, src))

    x_fm_v = x_fm.rearrange("(c p) t -> c p t", p=128)
    uc_agi_v = uc_agi.rearrange("q (c p) t -> q c p t", p=128)
    res_dr_v = res_dr.rearrange("(c p) t -> c p t", p=128)
    xn_dr_v = xn_dr.rearrange("(c p) t -> c p t", p=128)
    xnb_dr_v = xnb_dr.rearrange("(c p) t -> c p t", p=128)
    moe_dr_v = moe_dr.rearrange("(c p) t -> c p t", p=128)
    out_fm_v = out_fm.rearrange("(c p) t -> c p t", p=128)

    for o_, i_ in ((nrm, nrm_col), (cw, cw_col), (cbx, cb_col), (dtb, dtb_col),
                   (dpc, dp_col), (b1c, b1_col), (b2c, b2_col), (lnw, lnw_col),
                   (lnb, lnb_col), (sel, sel4), (ysl, ysel_c), (on1, ones1),
                   (onk, onesk), (idn, ident), (aneg, alog_pk), (wgt8, w_gate)):
        P.dma("sync", o_, i_)
    n_const = P.c("qy")

    P.vop("memset", cst_eps, EPS, inc="v")
    P.act(aneg, aneg, AF.Exp, waits=[("qy", n_const)], inc="s")
    P.vop("tensor_scalar_mul", aneg, aneg, -1.0, waits=P.after("s"), inc="v")

    wipb = arb.take(8 * 2 * DI)
    wxpb = arb.take(16 * 320)
    MB_P1 = arb.mark()
    wmb = arb.take(2048)
    wstg = arf.take(2048)

    w_ip_v = w_ip.rearrange("(c p) m -> c p m", p=128)
    w_xp_v = w_xp.rearrange("(c p) m -> c p m", p=128)

    for c in range(8):
        for h in range(2):
            P.dma("sync", wstg[:, 0:2048], w_ip_v[c, :, h * 2048:(h + 1) * 2048],
                  waits=P.after("v"))
            P.vop("tensor_scalar_mul",
                  wipb[:, c * 4096 + h * 2048:c * 4096 + (h + 1) * 2048],
                  wstg[:, 0:2048], nrm[:, c:c + 1],
                  waits=P.after("qy"), inc="v")
    for c in range(16):
        P.dma("sync", wstg[:, 0:320], w_xp_v[c], waits=P.after("s", "v"))
        P.scopy(ch(wxpb, c, 320), wstg[:, 0:320], waits=P.after("qy"), inc="s")
    P.dma("sync", wstg[0:DTR, 0:DSL], w_dt, waits=P.after("s"))
    P.scopy(wdtb, wstg[0:DTR, 0:DSL], waits=P.after("qy"), inc="s")

    n_rows = MOE_SL // 2048
    for r0 in range(0, n_rows, 128):
        nr = min(128, n_rows - r0)
        P.dma("sync", wstg[0:nr, 0:2048], w_moe_sl[r0:r0 + nr],
              waits=P.after("qg", "s"))
        P.scopy(wmb[0:nr, :], wstg[0:nr, 0:2048], waits=P.after("qy"), inc="s")
        P.dma("gpsimd", wm_agi[r0:r0 + nr], wmb[0:nr, :], waits=P.after("s"))

    def collective(kind, ins, outs, waits=()):
        def f(e, kind=kind, ins=ins, outs=outs):
            return e.collective_compute(
                kind, ALU.bypass, replica_groups=[list(range(NC))],
                ins=[ins.opt()], outs=[outs.opt()])
        P.step("gpsimd", f, waits=waits, inc="cc")

    collective("AllGather", wm_agi, wm_ago, waits=P.after("qg"))
    CC_WMOE = P.c("cc")
    wall_all()

    arf.reset(0)
    logsb_a = arf.take(TPC)
    logsb = logsb_a[0:8, :]
    MF_P1 = arf.mark()
    xt = arf.take(8 * W1)
    xnf = arf.take(8 * W1)
    ut = arf.take(16 * W1)
    convo = arf.take(T1)
    resst = arf.take(2 * W1)
    statr_a = arf.take(4 * W1)
    statr = statr_a[0:1, :]
    arb.reset(MB_P1)
    xn2b = arb.take(8 * W1)
    ucb = arb.take(16 * T1)
    xnbst = ucb[:, 0:8 * T1]
    dtrst = arb.take(T1)

    S_MU, S_RSTD, S_MR, S_V = 0, 1, 2, 3

    def srow(slot, off, n):
        return statr[:, slot * W1 + off:slot * W1 + off + n]

    for it in range(TPC // T1):
        t0 = it * T1
        halo = (it == 0)
        xo = 0 if halo else 3
        nnew = W1 if halo else T1
        src_off = t0 if halo else t0 + 3
        if not halo:
            uv = ut.rearrange("p (c w) -> p c w", w=W1)
            P.vop("tensor_copy", uv[:, :, 0:3], uv[:, :, T1:T1 + 3],
                  waits=P.after("t", "s"), inc="v")
        for c in range(8):
            P.dma("sync", ch(xt, c, W1)[:, xo:xo + nnew],
                  x_fm_v[c, :, src_off:src_off + nnew],
                  waits=(P.after("v", "s", "t") if c == 0 else ()))
        ldx = P.c("qy")

        subs = [(0, 512), (3, 512)] if halo else [(3, 512)]
        for off, n in subs:
            for c in range(8):
                P.mm(pb[7][0:1, 0:n], onk, ch(xt, c, W1)[:, off:off + n],
                     start=(c == 0), stop=(c == 7),
                     waits=(P.after("qy", "v", "s") if c == 0 else ()),
                     inc=("t" if c == 7 else None))
            for c in range(8):
                P.act(convo[:, 0:n], ch(xt, c, W1)[:, off:off + n], AF.Square,
                      waits=P.after("t"), inc="s")
                P.mm(pb[6][0:1, 0:n], onk, convo[:, 0:n],
                     start=(c == 0), stop=(c == 7), waits=P.after("s"),
                     inc="t")
            mu, rstd = srow(S_MU, off, n), srow(S_RSTD, off, n)
            mr, vv = srow(S_MR, off, n), srow(S_V, off, n)
            P.vop("tensor_scalar_mul", mu, pb[7][0:1, 0:n], 1.0 / D,
                  waits=P.after("t"), inc="v")
            P.vop("tensor_tensor", vv, mu, mu, ALU.mult, inc="v")
            P.vop("scalar_tensor_tensor", vv, pb[6][0:1, 0:n], 1.0 / D, vv,
                  ALU.mult, ALU.subtract, inc="v")
            P.act(rstd, vv, AF.Ln, bias=EPS, waits=P.after("v"), inc="s")
            P.act(rstd, rstd, AF.Exp, scale=-0.5, inc="s")
            P.vop("tensor_tensor", mr, mu, rstd, ALU.mult,
                  waits=P.after("s"), inc="v")
            P.mm(pb[5][:, 0:n], on1, rstd, True, True, waits=P.after("v"), inc="t")
            P.mm(pb[4][:, 0:n], on1, mr, True, True, inc="t")
            for c in range(8):
                P.vop("tensor_tensor", ch(xnf, c, W1)[:, off:off + n],
                      ch(xt, c, W1)[:, off:off + n], pb[5][:, 0:n], ALU.mult,
                      waits=(P.after("t", "qy") if c == 0 else ()), inc="v")
                P.vop("tensor_tensor", ch(xnf, c, W1)[:, off:off + n],
                      ch(xnf, c, W1)[:, off:off + n], pb[4][:, 0:n],
                      ALU.subtract, inc="v")
            for c in range(8):
                P.act(convo[:, 0:n], ch(xnf, c, W1)[:, off:off + n], AF.Square,
                      waits=P.after("v", "t"), inc="s")
                P.mm(pb[6][0:1, 0:n], onk, convo[:, 0:n],
                     start=(c == 0), stop=(c == 7), waits=P.after("s"),
                     inc="t")
            r2 = srow(S_RSTD, off, n)
            P.act(r2, pb[6][0:1, 0:n], AF.Ln, bias=EPS, scale=1.0 / D,
                  waits=P.after("t"), inc="s")
            P.act(r2, r2, AF.Exp, scale=-0.5, inc="s")
            P.mm(pb[5][:, 0:n], on1, r2, True, True,
                 waits=P.after("s", "v"), inc="t")
            for c in range(8):
                P.vop("tensor_tensor", ch(xn2b, c, W1)[:, off:off + n],
                      ch(xnf, c, W1)[:, off:off + n], pb[5][:, 0:n], ALU.mult,
                      waits=(P.after("t") if c == 0 else ()), inc="v")
            P.vop("tensor_copy", drn[:, 0:1],
                  ch(xn2b, 7, W1)[:, off + n - 1:off + n], inc="v")

        P.scopy(xnbst.rearrange("p (c w) -> p c w", w=T1),
                xnf.rearrange("p (c w) -> p c w", w=W1)[:, :, 3:3 + T1],
                waits=P.after("v", "qy"), inc="s")
        for c in range(8):
            P.dma("sync", xn_dr_v[c, :, t0:t0 + T1],
                  ch(xnf, c, W1)[:, 3:3 + T1],
                  waits=(P.after("v") if c == 0 else ()))
            P.dma("sync", xnb_dr_v[c, :, t0:t0 + T1], ch(xnbst, c, T1),
                  waits=(P.after("s") if c == 0 else ()))
        for c in range(8):
            P.mm(pb[7][0:8, 0:T1], wgt8[:, c * 8:(c + 1) * 8],
                 ch(xnf, c, W1)[:, 3:3 + T1], start=(c == 0), stop=(c == 7),
                 waits=(P.after("v") if c == 0 else ()),
                 inc=("t" if c == 7 else None))
        P.vop("tensor_copy", logsb[:, t0:t0 + T1], pb[7][0:8, 0:T1],
              waits=P.after("t"), inc="v")

        for m in range(32):
            bk = pb[5 + (m % 2)]
            for off, n in subs:
                for k in range(8):
                    P.mm(bk[:, 0:n],
                         wipb[:, k * 4096 + m * 128:k * 4096 + (m + 1) * 128],
                         ch(xn2b, k, W1)[:, off:off + n],
                         start=(k == 0), stop=(k == 7),
                         waits=(P.after("v", "s") if k == 0 else ()),
                         inc=("t" if k == 7 else None))
                if m < 16:
                    P.vop("tensor_copy", ch(ut, m, W1)[:, off:off + n],
                          bk[:, 0:n], waits=P.after("t"), inc="v")
                else:
                    half = 0 if (m % 2 == 0) else W1
                    P.scopy(resst[:, half + off:half + off + n],
                            bk[:, 0:n], waits=P.after("t", "qy"), inc="s")
            if m >= 16:
                half = 0 if (m % 2 == 0) else W1
                P.dma("sync", res_dr_v[m - 16, :, t0:t0 + T1],
                      resst[:, half + 3:half + 3 + T1], waits=P.after("s"))

        for c in range(16):
            P.vop("tensor_scalar_mul", convo, ch(ut, c, W1)[:, 0:T1],
                  cw[:, 4 * c:4 * c + 1], waits=P.after("v", "s"), inc="v")
            for k in (1, 2, 3):
                P.vop("scalar_tensor_tensor", convo,
                      ch(ut, c, W1)[:, k:k + T1], cw[:, 4 * c + k:4 * c + k + 1],
                      convo, ALU.mult, ALU.add, inc="v")
            P.act(ch(ucb, c, T1), convo, AF.Silu, bias=cbx[:, c:c + 1],
                  waits=P.after("v", "qy"), inc="s")
            P.dma("sync", uc_agi_v[it, c], ch(ucb, c, T1),
                  waits=P.after("s"))

        for mi, (mo, msz, bank) in enumerate(((0, 64, 5), (64, 128, 6),
                                              (192, 128, 5))):
            bk = pb[bank]
            for k in range(16):
                P.mm(bk[0:msz, 0:T1], wxpb[:, k * 320 + mo:k * 320 + mo + msz],
                     ch(ucb, k, T1), start=(k == 0), stop=(k == 15),
                     waits=(P.after("s", "v") if k == 0 else ()),
                     inc=("t" if k == 15 else None))
            if mi == 0:
                P.scopy(dtrst[0:DTR, 0:T1], bk[0:64, 0:T1],
                        waits=P.after("t", "qy"), inc="s")
                P.dma("sync", dtr_agi[it], dtrst[0:DTR, 0:T1],
                      waits=P.after("s"))
            else:
                half = 0 if mi == 1 else W1
                P.vop("tensor_copy", resst[:, half:half + T1], bk[0:128, 0:T1],
                      waits=P.after("t", "qy"), inc="v")
                P.dma("sync", bc_agi[it, (mi - 1) * 128:mi * 128, :],
                      resst[:, half:half + T1], waits=P.after("v"))

    q_p1_done = P.c("qy")
    cc_scan_in = {}

    arf.reset(MF_P1)
    tokm = arf.take(128)
    l2m = arf.take(128)
    is1m = arf.take(128)
    is2m = arf.take(128)
    m1r = arf.take(16)
    m2r = arf.take(16)
    qr = arf.take(16)
    w1r = arf.take(16)
    w2r = arf.take(16)
    combfm_a = arf.take(TPC)
    combfm = combfm_a[0:8, :]

    for c in range(16):
        P.mm(pb[7][:, 8 * c:8 * c + 8], logsb[:, 128 * c:128 * (c + 1)],
             idn[0:8, 0:8], True, True,
             waits=(P.after("s", "v") if c == 0 else ()),
             inc=("t" if c == 15 else None))
    P.vop("tensor_copy", tokm, pb[7][:, 0:128], waits=P.after("t"), inc="v")
    tv = tokm.rearrange("p (c e) -> p c e", e=8)
    P.vop("tensor_reduce", m1r, tv, AX.X, ALU.max, inc="v")
    m1b = m1r.rearrange("p (c o) -> p c o", o=1).broadcast_to([128, 16, 8])
    P.vop("tensor_tensor", is1m.rearrange("p (c e) -> p c e", e=8), tv, m1b,
          ALU.is_equal, inc="v")
    P.vop("scalar_tensor_tensor", l2m, is1m, -BIG, tokm, ALU.mult, ALU.add,
          inc="v")
    P.vop("tensor_reduce", m2r, l2m.rearrange("p (c e) -> p c e", e=8), AX.X,
          ALU.max, inc="v")
    m2b = m2r.rearrange("p (c o) -> p c o", o=1).broadcast_to([128, 16, 8])
    P.vop("tensor_tensor", is2m.rearrange("p (c e) -> p c e", e=8),
          l2m.rearrange("p (c e) -> p c e", e=8), m2b, ALU.is_equal, inc="v")
    P.vop("tensor_tensor", qr, m2r, m1r, ALU.subtract, inc="v")
    P.act(w1r, qr, AF.Sigmoid, scale=-1.0, waits=P.after("v"), inc="s")
    P.act(w2r, qr, AF.Sigmoid, scale=1.0, inc="s")
    P.vwait(P.after("s"))
    w1b = w1r.rearrange("p (c o) -> p c o", o=1).broadcast_to([128, 16, 8])
    w2b = w2r.rearrange("p (c o) -> p c o", o=1).broadcast_to([128, 16, 8])
    P.vop("tensor_tensor", is1m.rearrange("p (c e) -> p c e", e=8),
          is1m.rearrange("p (c e) -> p c e", e=8), w1b, ALU.mult, inc="v")
    P.vop("tensor_tensor", is2m.rearrange("p (c e) -> p c e", e=8),
          is2m.rearrange("p (c e) -> p c e", e=8), w2b, ALU.mult, inc="v")
    P.vop("tensor_tensor", is1m, is1m, is2m, ALU.add, inc="v")
    for cq in range(4):
        for j in range(4):
            c = 4 * cq + j
            P.mm(pb[5][0:8, 128 * j:128 * (j + 1)],
                 is1m[:, 8 * c:8 * c + 8], idn, True, True,
                 waits=(P.after("v", "s") if j == 0 else ()),
                 inc=("t" if j == 3 else None))
        P.vop("tensor_copy", combfm[:, 512 * cq:512 * (cq + 1)],
              pb[5][0:8, 0:512], waits=P.after("t"), inc="v")
    P.dma("gpsimd", comb_dr, combfm, waits=P.after("v"))

    wall_all()

    arf.reset(0)
    q1m = arf.take(TM)
    q1m2 = arf.take(TM)
    acc2 = [arf.take(8 * TM), arf.take(8 * TM)]
    combsb2 = [arf.take(TM), arf.take(TM)]
    arb.reset(0)
    xnb2 = [arb.take(8 * TM), arb.take(8 * TM)]
    wgb = arb.take(8 * FFN)
    wub = arb.take(8 * FFN)
    wdb = arb.take(12 * 1024)
    gsil = arb.take(TM)
    gsil2 = arb.take(TM)
    t2b = arb.take(12 * TM)

    s_cnt_gs = {}
    v_cnt_t2 = {}
    v_cnt_acc = {}
    t_gu_done = None
    t_dn_done = None
    q_str = None
    for half in range(2):
        for ti in range(2):
            tt = half * 2 + ti
            t0 = tt * TM
            for c in range(8):
                w0 = ()
                if ti == 0 and c == 0:
                    w0 = ([("t", t_gu_done)] if t_gu_done is not None
                          else P.after("t"))
                P.dma("sync", ch(xnb2[ti], c, TM), xnb_dr_v[c, :, t0:t0 + TM],
                      waits=w0)
        for e in range(NEXP + 1):
            off = e * EXP_W
            wg2 = moe_w(off, 1024, FFN).rearrange("(c p) m -> c p m", p=128)
            wu2 = moe_w(off + WG_SZ, 1024, FFN).rearrange(
                "(c p) m -> c p m", p=128)
            wd2 = moe_w(off + 2 * WG_SZ, FFN, 1024).rearrange(
                "(c p) m -> c p m", p=128)
            wts = ([("t", t_gu_done)] if t_gu_done is not None
                   else P.after("t")) + [("cc", CC_WMOE)]
            for k in range(8):
                P.dma("sync", ch(wgb, k, FFN), wg2[k],
                      waits=(wts if k == 0 else ()))
                P.dma("sync", ch(wub, k, FFN), wu2[k])
            for k in range(12):
                P.dma("sync", ch(wdb, k, 1024), wd2[k],
                      waits=([("t", t_dn_done)]
                             if (k == 0 and t_dn_done is not None) else ()))
            if e < NEXP:
                for ti in range(2):
                    tt = half * 2 + ti
                    P.dma("sync", combsb2[ti],
                          comb_dr[e:e + 1, tt * TM:(tt + 1) * TM]
                          .broadcast_to([128, TM]),
                          waits=(P.after("v") if ti == 0 else ()))
            ldt = P.c("qy")
            for ti in range(2):
                for mf in range(12):
                    par = mf % 2
                    bg, bu = (pb[0], pb[3]) if par == 0 else (pb[4], pb[5])
                    w_first = [("qy", ldt)]
                    if s_cnt_gs.get(par) is not None:
                        w_first.append(("s", s_cnt_gs[par]))
                        w_first.append(("v", v_cnt_t2[par]))
                    for k in range(8):
                        P.mm(bg[:, 0:TM],
                             wgb[:, k * FFN + mf * 128:k * FFN + (mf + 1) * 128],
                             ch(xnb2[ti], k, TM), start=(k == 0), stop=(k == 7),
                             waits=(w_first if k == 0 else ()))
                    for k in range(8):
                        P.mm(bu[:, 0:TM],
                             wub[:, k * FFN + mf * 128:k * FFN + (mf + 1) * 128],
                             ch(xnb2[ti], k, TM), start=(k == 0), stop=(k == 7),
                             inc=("t" if k == 7 else None))
                    gs = gsil if par == 0 else gsil2
                    q1 = q1m if par == 0 else q1m2
                    P.act(gs, bg[:, 0:TM], AF.Silu, waits=P.after("t"), inc="s")
                    s_cnt_gs[par] = P.c("s")
                    P.vop("tensor_tensor", q1, gs, bu[:, 0:TM], ALU.mult,
                          waits=P.after("s"), inc="v")
                    if e < NEXP:
                        P.vop("tensor_tensor", ch(t2b, mf, TM), q1,
                              combsb2[ti], ALU.mult, waits=[("qy", ldt)],
                              inc="v")
                    else:
                        P.vop("tensor_copy", ch(t2b, mf, TM), q1, inc="v")
                    v_cnt_t2[par] = P.c("v")
                P.vop("tensor_copy", drn[:, 0:1],
                      t2b[:, 12 * TM - 1:12 * TM], inc="v")
                v_all_t2 = P.c("v")
                if ti == 1:
                    t_gu_done = P.c("t")
                for mo in range(8):
                    par = mo % 2
                    bk = pb[1 + par]
                    w_first = [("v", v_all_t2)]
                    if v_cnt_acc.get(par) is not None:
                        w_first.append(("v", v_cnt_acc[par]))
                    for kc in range(12):
                        P.mm(bk[:, 0:TM],
                             wdb[:, kc * 1024 + mo * 128:kc * 1024 + (mo + 1) * 128],
                             ch(t2b, kc, TM), start=(kc == 0), stop=(kc == 11),
                             waits=(w_first if kc == 0 else ()),
                             inc=("t" if kc == 11 else None))
                    wv = [("t", P.c("t"))]
                    if e == 0 and mo == 0 and ti == 0 and q_str is not None:
                        wv.append(("qg", q_str))
                    if e == 0:
                        P.vop("tensor_copy", ch(acc2[ti], mo, TM),
                              bk[:, 0:TM], waits=wv, inc="v")
                    else:
                        P.vop("tensor_tensor", ch(acc2[ti], mo, TM),
                              ch(acc2[ti], mo, TM), bk[:, 0:TM], ALU.add,
                              waits=wv, inc="v")
                    v_cnt_acc[par] = P.c("v")
                if ti == 1:
                    t_dn_done = P.c("t")
            if half == 0:
                if e in (1, 3, 5, 7):
                    t = e // 2
                    collective("AllGather", uc_agi[t], uc_ago[t],
                               waits=([("qy", q_p1_done)] if t == 0 else ()))
                    cc_scan_in[("uc", t)] = P.c("cc")
                elif e == 8:
                    collective("AllGather", bc_agi[0], bc_ago[0])
                    collective("AllGather", dtr_agi[0], dtr_ago[0])
            elif e in (0, 1, 2):
                collective("AllGather", bc_agi[e + 1], bc_ago[e + 1])
                collective("AllGather", dtr_agi[e + 1], dtr_ago[e + 1])
        v_done = P.c("v")
        for ti in range(2):
            tt = half * 2 + ti
            for mo in range(8):
                P.dma("gpsimd", moe_dr_v[mo, :, tt * TM:(tt + 1) * TM],
                      ch(acc2[ti], mo, TM),
                      waits=([("v", v_done)] if (ti == 0 and mo == 0) else ()))
        q_str = P.c("qg")

    CC_SCANIN = P.c("cc")

    for t in range(4):
        def ucopy(e_, t=t):
            jD0, _ = _core_offs(e_)
            return e_.dma_start(out=uc_loc[t],
                                in_=uc_ago[t][:, ds(jD0, DSL), :])
        P.dyndma("sync", ucopy, waits=[("cc", cc_scan_in[("uc", t)])])
    wall_all()

    arf.reset(0)
    delta = arf.take(2 * TC)
    spa = arf.take(512)
    spb = arf.take(512)
    brep = arf.take(TC)
    crep = arf.take(TC)
    a_t2 = [arf.take(TC + 16), arf.take(TC + 16)]
    b_t = arf.take(TC + 16)
    h_t2 = [arf.take(TC + 16), arf.take(TC + 16)]
    spc = arf.take(16)
    yst = arf.take(512)
    arb.reset(0)
    ucs = arb.take(2 * TC)
    delb = arb.take(2 * TC)
    dub = arb.take(2 * TC)
    ypb2 = [arb.take(TC), arb.take(TC)]
    dtrsl_a = arb.take(TC)
    dtrsl = dtrsl_a[0:DTR, :]

    wstg4 = arf.take(2048)
    wopb = arb.take(16 * 1024)
    wf1b = arb.take(8 * 1024)
    wf2b = arb.take(8 * 1024)
    w_op_v = w_op.rearrange("(c p) m -> c p m", p=128)
    w_f1_v = w_f1.rearrange("(c p) m -> c p m", p=128)
    w_f2_v = w_f2.rearrange("(c p) m -> c p m", p=128)
    w4q = ([(wopb, w_op_v, c) for c in range(16)]
           + [b for c in range(8)
              for b in ((wf1b, w_f1_v, c), (wf2b, w_f2_v, c))])
    w4_state = {"n": 0}
    w4_scopy = []
    ws_free = [None, None]

    def w4_dma(k):
        for _ in range(k):
            n = w4_state["n"]
            if n >= len(w4q):
                return
            dstb, srcv, c = w4q[n]
            par = n % 2
            stg = wstg4[:, par * 1024:(par + 1) * 1024]
            w = [("s", ws_free[par])] if ws_free[par] is not None else []
            P.dma("sync", stg, srcv[c], waits=w)
            w4_scopy.append((ch(dstb, c, 1024), stg, P.c("qy"), par))
            w4_state["n"] = n + 1

    def w4_flush():
        for dst, stg, q, par in w4_scopy:
            P.scopy(dst, stg, waits=[("qy", q)], inc="s")
            ws_free[par] = P.c("s")
        w4_scopy.clear()

    P.vop("memset", a_t2[0][:, TC:TC + 1], 0.0, waits=P.after("s", "t"), inc="v")
    P.vop("memset", a_t2[1][:, TC:TC + 1], 0.0, inc="v")
    P.vop("memset", b_t[:, TC:TC + 1], 0.0, inc="v")
    pend_g = None

    db_banks = [pb[5], pb[3], pb[4]]
    du_banks = [pb[6], pb[2], pb[1], pb[0]]
    s_exp = {i: None for i in range(3)}
    v_bt = {i: None for i in range(4)}
    v_scan = {0: None, 1: None}
    s_yev = None
    q_ydma = None
    subidx = 0
    pend_ysel = None

    def emit_ysel(g, tc, v_ym):
        nonlocal s_yev, q_ydma
        for su in range(4):
            w = [("v", P.c("v"))] if su == 0 else []
            if su == 0 and s_yev is not None:
                w.append(("s", s_yev))
            P.mm(pb[7][32 * su:32 * su + 8, 0:512], ysl,
                 ypb2[g % 2][:, su * 512:(su + 1) * 512],
                 True, True, tp=(0, 32 * su), waits=w,
                 inc=("t" if su == 3 else None))
        wq = P.after("t")
        if q_ydma is not None:
            wq.append(("qy", q_ydma))
        P.scopy(yst[:, 0:512], pb[7][:, 0:512], waits=wq, inc="s")
        s_yev = P.c("s")
        for su in range(4):
            P.dma("sync",
                  y_agi[tc, 8 * g:8 * g + 8, su * 512:(su + 1) * 512],
                  yst[32 * su:32 * su + 8, 0:512],
                  waits=(P.after("s") if su == 0 else ()))
        q_ydma = P.c("qy")

    for tc in range(NTC):
        for t in range(4):
            w0 = ()
            if t == 0:
                w0 = P.after("s", "t")
                if tc == 0:
                    w0 = w0 + [("cc", CC_SCANIN)]
            P.dma("sync", dtrsl[:, t * T1:(t + 1) * T1], dtr_ago[t, tc],
                  waits=w0)
        for dd in range(2):
            for t in range(4):
                P.dma("sync", ch(ucs, dd, TC)[:, t * T1:(t + 1) * T1],
                      uc_loc[t, tc, dd * 128:(dd + 1) * 128, :],
                      waits=(P.after("v") if (dd == 0 and t == 0) else ()))
        for t in range(4):
            P.dma("sync", brep[:, t * T1:(t + 1) * T1], bc_ago[t, tc, 0:128, :],
                  waits=(P.after("v") if t == 0 else ()))
            P.dma("sync", crep[:, t * T1:(t + 1) * T1],
                  bc_ago[t, tc, 128:256, :])
        w4_flush()
        ld0 = P.c("qy")
        w4_dma(2)

        for dd in range(2):
            for su in range(4):
                par = subidx % 3
                subidx += 1
                bk = db_banks[par]
                w = [("qy", ld0)]
                if s_exp[par] is not None:
                    w.append(("s", s_exp[par]))
                w += P.after("v")
                P.mm(bk[:, 0:512], wdtb[:, dd * 128:(dd + 1) * 128],
                     dtrsl[:, su * 512:(su + 1) * 512], True, True,
                     waits=w, inc="t")
                dcol = delta[:, dd * TC + su * 512: dd * TC + (su + 1) * 512]
                P.act(spa, bk[:, 0:512], AF.Abs, bias=dtb[:, dd:dd + 1],
                      waits=P.after("t"), inc="s")
                P.act(spa, spa, AF.Exp, scale=-1.0, inc="s")
                P.act(spa, spa, AF.Ln, bias=1.0, inc="s")
                s_exp[par] = P.c("s")
                P.vop("tensor_scalar", spb, bk[:, 0:512], dtb[:, dd:dd + 1],
                      0.0, ALU.add, ALU.max, waits=P.after("s"), inc="v")
                P.vop("tensor_tensor", dcol, spa, spb, ALU.add, inc="v")
            dD = ch(delta, dd, TC)
            P.vop("tensor_tensor", ch(dub, dd, TC), dD, ch(ucs, dd, TC),
                  ALU.mult, waits=P.after("qy"), inc="v")
            P.vop("tensor_copy", ch(delb, dd, TC), dD, inc="v")
        P.vop("tensor_copy", drn[:, 0:1], dub[:, 2 * TC - 1:2 * TC], inc="v")
        P.vop("tensor_copy", drn[:, 1:2], delb[:, 2 * TC - 1:2 * TC], inc="v")
        v_dt = P.c("v")

        if tc == 0 or tc == NTC // 2:
            P.vop("memset", hst, 0.0, inc="v")
            pend_g = None

        for g in range(32):
            if g == 16:
                w4_flush()
                w4_dma(2)
            dd, gg = g // 16, g % 16
            Gb, cc_ = gg // 4, gg % 4
            tp = (32 * Gb, 0)
            selp = sel[32 * Gb:32 * Gb + 32, cc_ * 128:(cc_ + 1) * 128]
            at = a_t2[g % 2]
            for su in range(4):
                pd = subidx % 3
                pu = subidx % 4
                subidx += 1
                rng = slice(su * 512, (su + 1) * 512)
                wdb_ = [] if s_exp[pd] is None else [("s", s_exp[pd])]
                if su == 0:
                    wdb_.append(("v", v_dt))
                    if v_scan[g % 2] is not None:
                        wdb_.append(("v", v_scan[g % 2]))
                P.mm(db_banks[pd][:, 0:512], selp,
                     ch(delb, dd, TC)[32 * Gb:32 * Gb + 32, rng], True, True,
                     tp=tp, waits=wdb_, inc="t")
                wdu_ = [] if v_bt[pu] is None else [("v", v_bt[pu])]
                P.mm(du_banks[pu][:, 0:512], selp,
                     ch(dub, dd, TC)[32 * Gb:32 * Gb + 32, rng], True, True,
                     tp=tp, waits=wdu_, inc="t")
                P.act(at[:, rng], db_banks[pd][:, 0:512], AF.Exp,
                      scale=aneg[:, g:g + 1], waits=P.after("t"), inc="s")
                s_exp[pd] = P.c("s")
                P.vop("tensor_tensor", b_t[:, rng], du_banks[pu][:, 0:512],
                      brep[:, rng], ALU.mult, waits=P.after("t"), inc="v")
                v_bt[pu] = P.c("v")
            if pend_ysel is not None:
                flush_tc = pend_ysel[1]
                emit_ysel(*pend_ysel)
                pend_ysel = None
                if g == 0 and tc > 0:
                    collective("AllGather", y_agi[tc - 1],
                               y_ago[tc - 1], waits=P.after("qy"))
            P.vop("tensor_tensor_scan", h_t2[g % 2][:, 0:TC + 1],
                  at[:, 0:TC + 1], b_t[:, 0:TC + 1], hst[:, g:g + 1],
                  ALU.mult, ALU.add, waits=[("s", P.c("s"))], inc="v")
            v_scan[g % 2] = P.c("v")
            P.vop("memset", spc[:, 0:1], 0.0, inc="v")
            if pend_g is not None:
                P.vop("tensor_copy", hst[:, pend_g:pend_g + 1],
                      h_t2[pend_g % 2][:, TC - 1:TC], inc="v")
            pend_g = g
            P.vop("tensor_tensor", ypb2[g % 2], h_t2[g % 2][:, 0:TC], crep,
                  ALU.mult, inc="v")
            P.vop("tensor_copy", spc[:, 1:2], ypb2[g % 2][:, TC - 1:TC],
                  inc="v")
            v_ym = P.c("v")
            pend_ysel = (g, tc, v_ym)

    if pend_ysel is not None:
        emit_ysel(*pend_ysel)
        pend_ysel = None
    w4_flush()
    collective("AllGather", y_agi[NTC - 1],
               y_ago[NTC - 1], waits=P.after("qy"))
    CC_Y = P.c("cc")
    wall_all()

    arf.reset(0)
    y4d = [arf.take(16 * T4), arf.take(16 * T4)]
    res4d = [arf.take(16 * T4), arf.take(16 * T4)]
    o4d = [arf.take(8 * T4), arf.take(8 * T4)]
    xn4 = arf.take(8 * T4)
    sq4 = arf.take(T4)
    st4_a = arf.take(4 * T4)
    st4 = st4_a[0:1, :]
    arb.reset(0)
    uc4 = arb.take(16 * T4)
    ytbd = [arb.take(16 * T4), arb.take(16 * T4)]
    ob4 = arb.take(8 * T4)
    h1b = arb.take(8 * T4)

    def srow4(slot, n=T4):
        return st4[:, slot * T4:slot * T4 + n]

    NT4 = TPC // T4
    ld1 = {}
    ld2 = {}
    ld3 = {}
    qg_out = {}
    v_zdone = {}
    v_fin = {}

    def emit_L1(it):
        p = it % 2
        t0 = it * T4

        def ycopy(e, t0=t0):
            j = _core_j(e)
            return e.dma_start(out=y_loc[:, :, t0:t0 + T4],
                               in_=y_ago[ds(j, 1), :, :, t0:t0 + T4])
        P.dyndma("sync", ycopy, waits=([("cc", CC_Y)] if it == 0 else ()))
        cpy = P.c("qy")
        for c in range(16):
            w = [("qy", cpy)] if c == 0 else ()
            if c == 0 and it >= 2:
                w = w + [("qg", qg_out[it - 2])]
            P.dma("sync", ch(y4d[p], c, T4),
                  y_loc[c // 2, 128 * (c % 2):128 * (c % 2) + 128,
                        t0:t0 + T4], waits=w)
        for c in range(16):
            P.dma("sync", ch(res4d[p], c, T4), res_dr_v[c, :, t0:t0 + T4])
            P.dma("sync", ch(uc4, c, T4),
                  uc_agi_v[it // 2, c, :, (it % 2) * T4:(it % 2) * T4 + T4])
        ld1[it] = P.c("qy")

    def emit_L2(it, wv):
        t0 = it * T4
        for c in range(8):
            P.dma("sync", ch(xn4, c, T4), xn_dr_v[c, :, t0:t0 + T4],
                  waits=([("v", wv)] if (c == 0 and wv is not None) else ()))
        ld2[it] = P.c("qy")

    emit_L1(0)
    emit_L2(0, None)

    for it in range(NT4):
        p = it % 2
        t0 = it * T4
        bb = 0 if p == 0 else 4
        z4 = y4d[p]
        x4v = lambda c: res4d[p][:, c * T4:(c + 1) * T4]
        mo4v = lambda c: res4d[p][:, 8 * T4 + c * T4:8 * T4 + (c + 1) * T4]

        P.act(res4d[p], res4d[p], AF.Silu, waits=[("qy", ld1[it])], inc="s")
        s_sil = P.c("s")
        for c in range(16):
            P.vop("scalar_tensor_tensor", ch(y4d[p], c, T4), ch(uc4, c, T4),
                  dpc[:, c:c + 1], ch(y4d[p], c, T4), ALU.mult, ALU.add,
                  waits=([("qy", ld1[it])] if c == 0 else ()), inc="v")
        P.vop("tensor_tensor", ytbd[p], y4d[p], res4d[p], ALU.mult,
              waits=[("s", s_sil)], inc="v")
        P.vop("tensor_copy", drn[:, 0:1],
              ytbd[p][:, 16 * T4 - 1:16 * T4], inc="v")
        v_ytb = P.c("v")

        for c in range(8):
            P.dma("sync", x4v(c), x_fm_v[c, :, 3 + t0:3 + t0 + T4],
                  waits=([("v", v_ytb)] if c == 0 else ()))
        for c in range(8):
            P.dma("sync", mo4v(c), moe_dr_v[c, :, t0:t0 + T4])
        ld3[it] = P.c("qy")
        if it + 1 < NT4:
            emit_L1(it + 1)

        ope_v = {}
        for m in range(8):
            bk = pb[bb + (m % 2)]
            if m == 0:
                w0 = [("v", max(v_ytb, v_zdone.get(it - 2, 0)))]
            elif m >= 2:
                w0 = [("v", ope_v[m - 2])]
            else:
                w0 = ()
            for k in range(16):
                P.mm(bk[:, 0:T4],
                     wopb[:, k * 1024 + m * 128:k * 1024 + (m + 1) * 128],
                     ch(ytbd[p], k, T4), start=(k == 0), stop=(k == 15),
                     waits=(w0 if k == 0 else ()),
                     inc=("t" if k == 15 else None))
            t_op = P.c("t")
            P.vop("tensor_tensor", ch(o4d[p], m, T4), bk[:, 0:T4],
                  ch(xn4, m, T4), ALU.add,
                  waits=[("t", t_op)] + ([("qy", ld2[it])] if m == 0 else []),
                  inc="v")
            P.vop("tensor_copy", ch(ob4, m, T4), ch(o4d[p], m, T4), inc="v")
            ope_v[m] = P.c("v")
        P.vop("tensor_copy", drn[:, 0:1], ob4[:, 8 * T4 - 1:8 * T4], inc="v")
        v_ob_all = P.c("v")
        if it + 1 < NT4:
            emit_L2(it + 1, ope_v[7])

        s_gelu = {}
        for m in range(8):
            bk = pb[bb + (m % 2)]
            if m == 0:
                w0 = [("v", v_ob_all)]
            elif m >= 2:
                w0 = [("s", s_gelu[m - 2])]
            else:
                w0 = ()
            for k in range(8):
                P.mm(bk[:, 0:T4],
                     wf1b[:, k * 1024 + m * 128:k * 1024 + (m + 1) * 128],
                     ch(ob4, k, T4), start=(k == 0), stop=(k == 7),
                     waits=(w0 if k == 0 else ()),
                     inc=("t" if k == 7 else None))
            P.act(ch(h1b, m, T4), bk[:, 0:T4], AF.Gelu, bias=b1c[:, m:m + 1],
                  waits=[("t", P.c("t"))], inc="s")
            s_gelu[m] = P.c("s")
        s_gelu_all = P.c("s")

        zadd_v = {}
        for m in range(8):
            bk = pb[bb + (m % 2)]
            if m == 0:
                w0 = [("s", s_gelu_all)]
            elif m >= 2:
                w0 = [("v", zadd_v[m - 2])]
            else:
                w0 = ()
            for k in range(8):
                P.mm(bk[:, 0:T4],
                     wf2b[:, k * 1024 + m * 128:k * 1024 + (m + 1) * 128],
                     ch(h1b, k, T4), start=(k == 0), stop=(k == 7),
                     waits=(w0 if k == 0 else ()),
                     inc=("t" if k == 7 else None))
            P.vop("tensor_scalar_add", ch(z4, m, T4), bk[:, 0:T4],
                  b2c[:, m:m + 1], waits=[("t", P.c("t"))], inc="v")
            P.vop("tensor_tensor", ch(z4, m, T4), ch(z4, m, T4),
                  ch(o4d[p], m, T4), ALU.add, inc="v")
            zadd_v[m] = P.c("v")
        P.vop("tensor_copy", drn[:, 0:1], z4[:, 8 * T4 - 1:8 * T4], inc="v")
        v_zdone[it] = P.c("v")

        for c in range(8):
            P.mm(pb[2][0:1, 0:T4], onk, ch(z4, c, T4), start=(c == 0),
                 stop=(c == 7),
                 waits=([("v", v_zdone[it])] if c == 0 else ()),
                 inc=("t" if c == 7 else None))
        for c in range(8):
            P.act(sq4, ch(z4, c, T4), AF.Square, waits=P.after("t"), inc="s")
            P.mm(pb[3][0:1, 0:T4], onk, sq4, start=(c == 0), stop=(c == 7),
                 waits=P.after("s"), inc="t")
        mu4, v4 = srow4(0), srow4(1)
        r4, mr4 = srow4(2), srow4(3)
        P.vop("tensor_scalar_mul", mu4, pb[2][0:1, 0:T4], 1.0 / D,
              waits=P.after("t"), inc="v")
        P.vop("tensor_tensor", v4, mu4, mu4, ALU.mult, inc="v")
        P.vop("scalar_tensor_tensor", v4, pb[3][0:1, 0:T4], 1.0 / D, v4,
              ALU.mult, ALU.subtract, inc="v")
        P.act(r4, v4, AF.Ln, bias=EPS, waits=P.after("v"), inc="s")
        P.act(r4, r4, AF.Exp, scale=-0.5, inc="s")
        P.vop("tensor_tensor", mr4, mu4, r4, ALU.mult, waits=P.after("s"),
              inc="v")
        P.mm(pb[6][:, 0:T4], on1, r4, True, True, waits=P.after("v"), inc="t")
        P.mm(pb[7][:, 0:T4], on1, mr4, True, True, inc="t")
        t_bc = P.c("t")
        for c in range(8):
            w0 = []
            if c == 0:
                w0 = [("t", t_bc), ("qy", ld3[it])]
            P.vop("tensor_tensor", ch(z4, c, T4), ch(z4, c, T4),
                  pb[6][:, 0:T4], ALU.mult, waits=w0, inc="v")
            P.vop("tensor_tensor", ch(z4, c, T4), ch(z4, c, T4),
                  pb[7][:, 0:T4], ALU.subtract, inc="v")
            P.vop("tensor_scalar", ch(z4, c, T4), ch(z4, c, T4),
                  lnw[:, c:c + 1], lnb[:, c:c + 1], ALU.mult, ALU.add,
                  inc="v")
        P.vop("tensor_tensor", z4[:, 0:8 * T4], z4[:, 0:8 * T4],
              res4d[p][:, 0:8 * T4], ALU.add, inc="v")
        P.vop("tensor_tensor", z4[:, 0:8 * T4], z4[:, 0:8 * T4],
              res4d[p][:, 8 * T4:16 * T4], ALU.add, inc="v")
        for c in range(8):
            P.dma("gpsimd", out_fm_v[c, :, t0:t0 + T4], ch(z4, c, T4),
                  waits=(P.after("v") if c == 0 else ()))
        qg_out[it] = P.c("qg")
        v_fin[it] = P.c("v")

    for t, src in taps:
        P.dma("gpsimd", t, src, waits=P.after("qg", "v", "s", "t"))

    P.step("gpsimd", None, waits=P.after("qg", "qy"))
    P.step("sync", None, waits=P.after("qg"))

    with es, P.es:
        with nc.Block() as block:
            P.emit(block)

    return nc



def _prep_inputs(a):
    import ml_dtypes
    bf16 = ml_dtypes.bfloat16
    f32 = np.float32

    x = np.asarray(a["L0_attn_x"], f32)
    norm_w = np.asarray(a["L0_attn_norm_w"], f32)
    in_proj_w = np.asarray(a["L0_attn_in_proj_w"], f32)
    conv_w = np.asarray(a["L0_attn_conv_w"], f32)
    conv_b = np.asarray(a["L0_attn_conv_b"], f32)
    x_proj_w = np.asarray(a["L0_attn_x_proj_w"], f32)
    dt_w = np.asarray(a["L0_attn_dt_w"], f32)
    dt_b = np.asarray(a["L0_attn_dt_b"], f32)
    A_log = np.asarray(a["L0_attn_A_log"], f32)
    Dp = np.asarray(a["L0_attn_Dp"], f32)
    out_proj_w = np.asarray(a["L0_attn_out_proj_w"], f32)
    ff_w1 = np.asarray(a["L0_attn_ff_w1"], f32)
    ff_b1 = np.asarray(a["L0_attn_ff_b1"], f32)
    ff_w2 = np.asarray(a["L0_attn_ff_w2"], f32)
    ff_b2 = np.asarray(a["L0_attn_ff_b2"], f32)
    ff_ln_w = np.asarray(a["L0_attn_ff_ln_w"], f32)
    ff_ln_b = np.asarray(a["L0_attn_ff_ln_b"], f32)
    gate_w = np.asarray(a["L0_ffn_gate_w"], f32)
    w_up = np.asarray(a["L0_ffn_w_up"], f32)
    w_down = np.asarray(a["L0_ffn_w_down"], f32)
    sg_gate = np.asarray(a["L0_ffn_sg_gate"], f32)
    sg_up = np.asarray(a["L0_ffn_sg_up"], f32)
    sg_down = np.asarray(a["L0_ffn_sg_down"], f32)

    C = np.ascontiguousarray

    def col(v, nchunk):
        return C(v.reshape(nchunk, 128).T.astype(f32))

    shared = {
        "w_ip": C(in_proj_w.T),
        "nrm_col": col(norm_w, 8),
        "cw_col": C(conv_w.reshape(16, 128, 4).transpose(1, 0, 2).reshape(128, 64)),
        "cb_col": col(conv_b, 16),
        "dp_col": col(Dp, 16),
        "w_op": C(out_proj_w.T),
        "w_f1": C(ff_w1.T), "b1_col": col(ff_b1, 8),
        "w_f2": C(ff_w2.T), "b2_col": col(ff_b2, 8),
        "lnw_col": col(ff_ln_w, 8), "lnb_col": col(ff_ln_b, 8),
        "w_gate": C(gate_w.T.reshape(8, 128, 8).transpose(1, 0, 2).reshape(128, 64)),
        "ones1": np.ones((1, 128), f32),
        "onesk": np.ones((128, 1), f32),
        "ident": np.eye(128, dtype=f32),
    }
    xp_wide = np.concatenate([x_proj_w[0:64],
                              np.tile(x_proj_w[64:80], (8, 1)),
                              np.tile(x_proj_w[80:96], (8, 1))], axis=0)
    shared["w_xp"] = C(xp_wide.T)
    base = np.zeros((32, 512), f32)
    for c in range(4):
        for p in range(128):
            base[8 * c + p // 16, c * 128 + p] = 1.0
    shared["sel4"] = C(np.tile(base, (4, 1)).astype(bf16))
    ys = np.zeros((128, 8), f32)
    for p in range(128):
        ys[p, p // 16] = 1.0
    shared["ysel_c"] = C(ys.astype(bf16))

    parts = []
    for e in range(NEXP):
        parts.append(C(w_up[e, :FFN, :].T).ravel())
        parts.append(C(w_up[e, FFN:, :].T).ravel())
        parts.append(C(w_down[e].T).ravel())
    parts.append(C(sg_gate.T).ravel())
    parts.append(C(sg_up.T).ravel())
    parts.append(C(sg_down.T).ravel())
    moe_flat = np.concatenate(parts).astype(f32)
    assert moe_flat.size == MOE_TOT

    in_maps = []
    for j in range(NC):
        b, p0 = j // 4, (j % 4) * TPC
        xs = np.zeros((D, TPC + 3), f32)
        if p0 == 0:
            xs[:, 3:] = x[b, 0:TPC].T
        else:
            xs[:, :] = x[b, p0 - 3:p0 + TPC].T
        sl = slice(DSL * j, DSL * (j + 1))
        al = A_log[sl].reshape(32, 8, 16).transpose(1, 2, 0).reshape(128, 32)
        m = dict(shared)
        m["x_fm"] = xs
        m["w_dt"] = C(dt_w[sl].T)
        m["dtb_col"] = col(dt_b[sl], 2)
        m["alog_pk"] = C(al)
        m["w_moe_sl"] = C(moe_flat[j * MOE_SL:(j + 1) * MOE_SL]
                          .reshape(MOE_SL // 2048, 2048))
        in_maps.append(m)
    return in_maps


_NC_CACHE = {}


def kernel(**inputs):
    in_maps = _prep_inputs(inputs)
    if "prog" not in _NC_CACHE:
        _NC_CACHE["prog"] = build_program(debug=False)
    nc = _NC_CACHE["prog"]
    res = run_bass_kernel_spmd(nc, in_maps, core_ids=list(range(NC)))
    out = np.empty((B, L, D), np.float32)
    for j in range(NC):
        b, p0 = j // 4, (j % 4) * TPC
        out[b, p0:p0 + TPC, :] = res.results[j]["out_fm"].T
    return out
